# version 17
# baseline (speedup 1.0000x reference)
import contextlib
import os
import numpy as np
import ml_dtypes
import concourse.bass as bass
import concourse.mybir as mybir
from concourse.bass_utils import run_bass_kernel_spmd

if os.environ.get("BASS_SIM_FP64") == "1":
    import concourse.bass_interp as _bi
    if not getattr(_bi.MultiCoreSim, "_nonfinite_ok_patch", False):
        _orig_mcs_init = _bi.MultiCoreSim.__init__

        def _mcs_init(self, *a, **kw):
            kw.setdefault("require_finite", False)
            kw.setdefault("require_nnan", False)
            _orig_mcs_init(self, *a, **kw)
        _bi.MultiCoreSim.__init__ = _mcs_init
        _bi.MultiCoreSim._nonfinite_ok_patch = True

F32 = mybir.dt.float32
BF16 = mybir.dt.bfloat16
I8 = mybir.dt.int8
ALU = mybir.AluOpType
ACT = mybir.ActivationFunctionType
AXX = mybir.AxisListType.X

B, T, D, I = 8, 4096, 1024, 2816
NCORES = 8
DCH = D // 128
ICH = I // 128
TTILES = T // 128
TC = 256
NCH = T // TC
FTC = 512
NFC = T // FTC
EPS = 1e-6
EPS_LN = 1e-5
BETA_ = 0.3
EXP10 = float(np.exp(np.float32(10.0)).astype(np.float32))
INV127 = float(np.float32(1.0) / np.float32(127.0))
INVD = float(np.float32(1.0) / np.float32(D))

SEMS = ["cv", "cs", "ct", "cp",
        "dxa", "dxb", "dq1", "dq1b", "dq2", "dq2b", "dq3", "dsc",
        "dxn0", "dxn1", "dw0", "dw1",
        "dcx0", "dcx1", "dwg0", "dwg1",
        "dhr0", "dhr1", "dsp0", "dsp1", "dwd0", "dwd1",
        "gacc", "grow", "gwq", "gcc",
        "gsp0", "gsp1", "gh0", "gh1", "go0", "go1"]


class Rec:
    def __init__(self):
        self.ops = []
        self.cnt = {s: 0 for s in SEMS}

    def op(self, eng, fn, inc=None, amt=1):
        if inc is not None:
            self.cnt[inc] += amt
            self.ops.append((eng, fn, inc, amt))
        else:
            self.ops.append((eng, fn, None, 0))

    def val(self, name):
        return self.cnt[name]

    def wait(self, eng, name, v=None):
        if v is None:
            v = self.cnt[name]
        assert v <= self.cnt[name], (name, v, self.cnt[name])
        if v <= 0:
            return

        def f(e, _n=name, _v=v):
            e.wait_ge(f.handles[_n], _v)
        self.ops.append((eng, f, None, 0))

    def replay(self, eng_key, engine, handles):
        for (k, fn, inc, amt) in self.ops:
            if k != eng_key:
                continue
            fn.handles = handles
            inst = fn(engine)
            if inc is not None:
                inst.then_inc(handles[inc], amt)


_WITH_FENCES = os.environ.get("BASS_SIM_FP64") == "1"


def build_program():
    nc = bass.Bass("TRN2", target_bir_lowering=False)

    d_x = nc.dram_tensor("x", [T, D], F32, kind="ExternalInput").ap()
    d_wstack = nc.dram_tensor("wstack", [4, DCH, 128, DCH, 128], F32, kind="ExternalInput").ap()
    d_gd = nc.dram_tensor("gdraw", [128, DCH], F32, kind="ExternalInput").ap()
    d_mix = nc.dram_tensor("mixv", [128, DCH], F32, kind="ExternalInput").ap()
    d_clipx = nc.dram_tensor("clipx", [1, 1], F32, kind="ExternalInput").ap()
    d_clipd = nc.dram_tensor("clipd", [1, 1], F32, kind="ExternalInput").ap()
    d_wg_nat = nc.dram_tensor("wg_nat", [ICH, 128, D], F32, kind="ExternalInput").ap()
    d_wu_nat = nc.dram_tensor("wu_nat", [ICH, 128, D], F32, kind="ExternalInput").ap()
    d_wd_nat = nc.dram_tensor("wd_nat", [DCH, 128, I], F32, kind="ExternalInput").ap()
    d_wgT = nc.dram_tensor("wgT", [DCH, 128, I], F32, kind="ExternalInput").ap()
    d_wuT = nc.dram_tensor("wuT", [DCH, 128, I], F32, kind="ExternalInput").ap()
    d_wdT = nc.dram_tensor("wdT", [ICH, 128, D], F32, kind="ExternalInput").ap()
    d_id = nc.dram_tensor("idm", [128, 128], F32, kind="ExternalInput").ap()
    d_out = nc.dram_tensor("out", [T, D], F32, kind="ExternalOutput").ap()

    s_xn = nc.dram_tensor("xn_s", [DCH, 128, T], F32).ap()
    s_spk = nc.dram_tensor("spk_s", [DCH, 128, T], BF16).ap()
    s_h = nc.dram_tensor("h_s", [TTILES, 128, ICH, 128], F32).ap()
    s_wqg = nc.dram_tensor("wqg_s", [ICH, DCH, 128, 128], BF16).ap()
    s_wqu = nc.dram_tensor("wqu_s", [ICH, DCH, 128, 128], BF16).ap()
    s_wqd = nc.dram_tensor("wqd_s", [ICH, 128, D], BF16).ap()
    s_rows = nc.dram_tensor("rows_s", [8, I], F32).ap()
    cc1_in = nc.dram_tensor("cc1_in", [1, 1], F32).ap()
    cc1_out = nc.dram_tensor("cc1_out", [1, 1], F32).ap()
    cc2_in = nc.dram_tensor("cc2_in", [1, 1], F32).ap()
    cc2_out = nc.dram_tensor("cc2_out", [1, 1], F32).ap()

    rec = Rec()

    es = contextlib.ExitStack()
    with es:
        handles = {}
        for s in SEMS:
            handles[s] = es.enter_context(nc.semaphore(s))

        def mksb(stack, name, shape, dt=F32):
            return stack.enter_context(nc.sbuf_tensor(name, shape, dt))

        t_id = mksb(es, "t_id", [128, 128])
        t_mix = mksb(es, "t_mix", [128, DCH])
        t_imix = mksb(es, "t_imix", [128, DCH])
        t_gd = mksb(es, "t_gd", [128, DCH])
        t_scal = mksb(es, "t_scal", [1, 16])
        t_invx = mksb(es, "t_invx", [128, 1])
        t_invh = mksb(es, "t_invh", [128, 1])
        t_sxcol = mksb(es, "t_sxcol", [128, 1])
        t_shcol = mksb(es, "t_shcol", [128, 1])
        t_absx = mksb(es, "t_absx", [128, 1])
        t_absh = mksb(es, "t_absh", [128, 1])
        t_tmpc = mksb(es, "t_tmpc", [128, 1])
        t_tmp4 = mksb(es, "t_tmp4", [128, 4])
        t_rowred = mksb(es, "t_rowred", [1, 128])
        t_sg = mksb(es, "t_sg", [128, ICH])
        t_su = mksb(es, "t_su", [128, ICH])
        t_sd = mksb(es, "t_sd", [128, DCH])
        t_sg2 = mksb(es, "t_sg2", [128, ICH])
        t_su2 = mksb(es, "t_su2", [128, ICH])
        t_sinv = mksb(es, "t_sinv", [128, ICH])
        t_sinv2 = mksb(es, "t_sinv2", [128, ICH])
        t_sinv3 = mksb(es, "t_sinv3", [128, ICH])
        t_sinvT = mksb(es, "t_sinvT", [ICH, 128])
        t_zero8 = mksb(es, "t_zero8", [128, DCH])
        t_epsln = mksb(es, "t_epsln", [128, 1])
        t_mag = mksb(es, "t_mag", [128, 1])
        t_nmag = mksb(es, "t_nmag", [128, 1])

        pp = [es.enter_context(nc.psum_tensor(f"pp{i}", [128, 512], F32)) for i in range(8)]
        p_c = pp[:4]
        p_b = [pp[i][:, 0:TC] for i in range(4)]
        p_tp = [pp[4 + i][:, 0:128] for i in range(2)]

        block = es.enter_context(nc.Block())

        def dma(eng_q, sem, out_ap, in_ap, transpose=False):
            if transpose:
                def f(e, o=out_ap, i=in_ap):
                    return e.dma_start_transpose(out=o, in_=i)
            else:
                def f(e, o=out_ap, i=in_ap):
                    return e.dma_start(out=o, in_=i)
            rec.op(eng_q, f, inc=sem, amt=16)

        def vop(fn, inc=False):
            rec.op("v", fn, inc=("cv" if inc else None))

        def sop(fn, inc=False):
            rec.op("s", fn, inc=("cs" if inc else None))

        def top(fn, inc=False):
            rec.op("t", fn, inc=("ct" if inc else None))

        def fence(eng):
            rec.op(eng, lambda e: e.drain())

        def nfence(eng):
            if _WITH_FENCES:
                rec.op(eng, lambda e: e.drain())
            else:
                rec.op(eng, lambda e: e.engine_nop())

        def fence_inc(eng, sem):
            rec.op(eng, lambda e: e.drain(), inc=sem)

        def barrier():
            fence_inc("v", "cv")
            fence_inc("s", "cs")
            fence_inc("t", "ct")
            fence_inc("g", "cp")
            tg = [(s, rec.val(s)) for s in SEMS]
            for eng in ["v", "s", "t", "g", "sync"]:
                for (sn, sv) in tg:
                    rec.wait(eng, sn, sv)

        dma("sync", "dsc", t_id[:], d_id)
        dma("sync", "dsc", t_gd[:], d_gd)
        dma("sync", "dsc", t_mix[:], d_mix)
        v0 = rec.val("dsc")
        rec.wait("v", "dsc", v0)
        vop(lambda e: e.tensor_scalar(t_imix[:], t_mix[:], -1.0, 1.0, ALU.mult, ALU.add))
        vop(lambda e: e.memset(t_zero8[:], 0.0))
        vop(lambda e: e.memset(t_absx[:], 0.0))
        vop(lambda e: e.memset(t_absh[:], 0.0))
        vop(lambda e: e.memset(t_epsln[:], EPS_LN))
        vop(lambda e: e.memset(t_mag[:], 12582912.0))
        vop(lambda e: e.memset(t_nmag[:], -12582912.0))
        fence_inc("v", "cv")
        rec.wait("s", "dsc", v0)
        sop(lambda e: e.activation(out=t_gd[:], in_=t_gd[:], func=ACT.Exp), inc=True)
        ev_gd = rec.val("cs")

        with contextlib.ExitStack() as esA:
            a_x = [mksb(esA, f"a_x{i}", [128, 4, D]) for i in range(2)]
            a_sq = mksb(esA, "a_sq", [128, 4, D])
            a_xn = mksb(esA, "a_xn", [128, 4, D])
            a_cm = mksb(esA, "a_cm", [128, 4])
            a_cv = mksb(esA, "a_cv", [128, 4])
            a_sd = mksb(esA, "a_sd", [128, 4])
            a_rs = mksb(esA, "a_rs", [128, 4])
            a_acc = mksb(esA, "a_acc", [128, DCH, 512])
            q_nat = [mksb(esA, f"q_nat{i}", [128, 4, D]) for i in range(2)]
            q_strip = [mksb(esA, f"q_strip{i}", [128, I]) for i in range(2)]
            q_rep = mksb(esA, "q_rep", [128, I])
            q_rep2 = mksb(esA, "q_rep2", [128, I])
            q_rep3 = mksb(esA, "q_rep3", [128, D])
            q_f = mksb(esA, "q_f", [128, I])
            q_i8 = mksb(esA, "q_i8", [128, I], I8)
            q_b = mksb(esA, "q_b", [128, I], BF16)

            ev_xn_v = {}
            ev_tp = {}
            ev_acc_dma = {}

            def gen_ln():
              for ib in range(TTILES // 4):
                par = ib % 2
                xsem = "dxa" if par == 0 else "dxb"
                if ib >= 2:
                    rec.wait("sync", "cv", ev_xn_v[ib - 2])
                dma("sync", xsem, a_x[par][:],
                    d_x[ib * 512:(ib + 1) * 512, :].rearrange("(s p) d -> p s d", p=128))
                v_in = rec.val(xsem)
                rec.wait("s", xsem, v_in)
                if ib >= 1:
                    rec.wait("s", "cv", rec.val("cv"))
                sop(lambda e, p=par: e.activation(out=a_sq[:], in_=a_x[p][:], func=ACT.Square), inc=True)
                v_sq = rec.val("cs")
                rec.wait("v", xsem, v_in)
                rec.wait("v", "cs", v_sq)
                for i4 in range(4):
                    vop(lambda e, p=par, i4=i4: e.tensor_reduce(a_cm[:, i4:i4 + 1], a_x[p][:, i4, :], AXX, ALU.add))
                    vop(lambda e, i4=i4: e.tensor_reduce(a_cv[:, i4:i4 + 1], a_sq[:, i4, :], AXX, ALU.add))
                fence("v")
                vop(lambda e: e.tensor_single_scalar(a_cm[:], a_cm[:], INVD, ALU.mult))
                vop(lambda e: e.tensor_single_scalar(a_cv[:], a_cv[:], INVD, ALU.mult))
                fence("v")
                vop(lambda e: e.tensor_tensor(a_sd[:], a_cm[:], a_cm[:], ALU.mult))
                fence("v")
                vop(lambda e: e.tensor_tensor(a_cv[:], a_cv[:], a_sd[:], ALU.subtract), inc=True)
                v_var = rec.val("cv")
                rec.wait("s", "cv", v_var)
                sop(lambda e: e.activation(out=a_sd[:], in_=a_cv[:], func=ACT.Sqrt, bias=t_epsln[:]), inc=True)
                rec.wait("v", "cs", rec.val("cs"))
                vop(lambda e: e.reciprocal(a_rs[:], a_sd[:]))
                fence("v")
                if ib >= 1:
                    rec.wait("v", "ct", ev_tp[ib - 1])
                for i4 in range(4):
                    vop(lambda e, p=par, i4=i4: e.tensor_scalar(
                        a_xn[:, i4, :], a_x[p][:, i4, :], a_cm[:, i4:i4 + 1], a_rs[:, i4:i4 + 1],
                        ALU.subtract, ALU.mult), inc=(i4 == 3))
                v_xn = rec.val("cv")
                fence("v")
                vop(lambda e: e.tensor_reduce(t_tmpc[:], a_xn[:].rearrange("p s d -> p (s d)"), AXX, ALU.max, apply_absolute_value=True))
                fence("v")
                vop(lambda e: e.tensor_tensor(t_absx[:], t_absx[:], t_tmpc[:], ALU.max), inc=True)
                ev_xn_v[ib] = rec.val("cv")
                rec.wait("t", "cv", v_xn)
                if ib >= 1:
                    rec.wait("s", "gacc", ev_acc_dma[ib - 1])
                for i4 in range(4):
                    for g in range(DCH):
                        sl = (i4 * DCH + g) % 2
                        rec.wait("t", "cs", rec.val("cs"))
                        top(lambda e, g=g, i4=i4, sl=sl: e.transpose(
                            p_tp[sl], a_xn[:, i4, g * 128:(g + 1) * 128], t_id[:]), inc=True)
                        v_tp = rec.val("ct")
                        rec.wait("s", "ct", v_tp)
                        sop(lambda e, g=g, sl=sl, col=i4 * 128:
                            e.copy(out=a_acc[:, g, col:col + 128], in_=p_tp[sl]), inc=True)
                ev_tp[ib] = rec.val("ct")
                v_acc = rec.val("cs")
                rec.wait("g", "cs", v_acc)
                dma("g", "gacc", s_xn[:, :, ib * 512:(ib + 1) * 512].rearrange("g p t -> p g t"), a_acc[:])
                ev_acc_dma[ib] = rec.val("gacc")
                yield

              fence("v")
              vop(lambda e: e.tensor_copy(t_tmpc[:], t_absx[:]), inc=True)
              rec.wait("t", "cv", rec.val("cv"))
              rec.wait("t", "cs", rec.val("cs"))
              top(lambda e: e.transpose(p_tp[0][0:1, :], t_tmpc[:], t_id[:]), inc=True)
              rec.wait("s", "ct", rec.val("ct"))
              sop(lambda e: e.copy(out=t_rowred[:], in_=p_tp[0][0:1, :]), inc=True)
              rec.wait("v", "cs", rec.val("cs"))
              vop(lambda e: e.tensor_reduce(t_scal[:, 0:1], t_rowred[:], AXX, ALU.max), inc=True)
              rec.wait("g", "cv", rec.val("cv"))
              dma("g", "gcc", cc1_in, t_scal[:, 0:1])
              rec.wait("g", "gcc", rec.val("gcc"))

              def f_cc1(e):
                  return e.collective_compute(
                      "AllReduce", ALU.max,
                      replica_groups=[list(range(NCORES))],
                      ins=[cc1_in.opt()], outs=[cc1_out.opt()])
              rec.op("g", f_cc1, inc="cp")
              v_cc1done[0] = rec.val("cp")

            ev_natv = {}
            ev_stripv = {}
            _gi = [0]
            _hi = [0]
            v_cc1done = [None]

            def qw_scales(nat_ap, n_nat, scol, sinv):
                nb = (n_nat + 3) // 4
                for ib4 in range(nb):
                    i0 = ib4 * 4
                    cnt = min(4, n_nat - i0)
                    gi = _gi[0]; _gi[0] += 1
                    npar = gi % 2
                    nsem = "dq1" if npar == 0 else "dq1b"
                    if gi >= 2:
                        rec.wait("sync", "cv", ev_natv[gi - 2])
                    dma("sync", nsem, q_nat[npar][:, 0:cnt, :],
                        nat_ap[i0:i0 + cnt].rearrange("c p d -> p c d"))
                    rec.wait("v", nsem, rec.val(nsem))
                    for i4 in range(cnt):
                        vop(lambda e, p=npar, i4=i4: e.tensor_reduce(
                            t_tmp4[:, i4:i4 + 1], q_nat[p][:, i4, :], AXX, ALU.max, apply_absolute_value=True),
                            inc=(i4 == cnt - 1))
                    ev_natv[gi] = rec.val("cv")
                    fence("v")
                    vop(lambda e, i0=i0, cnt=cnt, sc=scol: e.tensor_single_scalar(
                        sc[:, i0:i0 + cnt], t_tmp4[:, 0:cnt], INV127, ALU.mult))
                    fence("v")
                    vop(lambda e, i0=i0, cnt=cnt, si=sinv, sc=scol: e.reciprocal(
                        si[:, i0:i0 + cnt], sc[:, i0:i0 + cnt]))
                    yield
                fence_inc("v", "cv")

            def qw_scales_wide(nat_list, scol, sinv):
                for icx in range(len(nat_list)):
                    hi = _hi[0]; _hi[0] += 1
                    spar = hi % 2
                    ssem = "dq2" if spar == 0 else "dq2b"
                    if hi >= 2:
                        rec.wait("sync", "cv", ev_stripv[hi - 2])
                    dma("sync", ssem, q_strip[spar][:, 0:I], nat_list[icx])
                    rec.wait("v", ssem, rec.val(ssem))
                    vop(lambda e, p=spar: e.tensor_reduce(t_tmpc[:], q_strip[p][:, 0:I], AXX, ALU.max, apply_absolute_value=True), inc=True)
                    ev_stripv[hi] = rec.val("cv")
                    fence("v")
                    vop(lambda e, icx=icx, sc=scol: e.tensor_single_scalar(sc[:, icx:icx + 1], t_tmpc[:], INV127, ALU.mult))
                    fence("v")
                    vop(lambda e, icx=icx, si=sinv, sc=scol: e.reciprocal(si[:, icx:icx + 1], sc[:, icx:icx + 1]))
                    yield
                fence_inc("v", "cv")

            def qw_row(sinv, row_idx, n_nat, free_T):
                rec.wait("t", "cv", rec.val("cv"))
                rec.wait("t", "cs", rec.val("cs"))
                top(lambda e, n=n_nat, si=sinv: e.transpose(p_tp[0][0:n, :], si[:, 0:n], t_id[:]), inc=True)
                rec.wait("s", "ct", rec.val("ct"))
                rec.wait("s", "grow", rec.val("grow"))
                sop(lambda e, n=n_nat: e.copy(out=t_sinvT[0:n, :], in_=p_tp[0][0:n, :]), inc=True)
                rec.wait("g", "cs", rec.val("cs"))
                dma("g", "grow", s_rows[row_idx, 0:free_T].rearrange("(c p) -> c p", p=128), t_sinvT[0:n_nat, :])

            def qw_strips(T_list, row_idx, free_T, dst_scatter, q_repx):
                rec.wait("sync", "grow", rec.val("grow"))
                dma("sync", "dq3", q_repx[:, 0:free_T], s_rows[row_idx:row_idx + 1, 0:free_T].broadcast_to((128, free_T)))
                v_rep = rec.val("dq3")
                for hgx in range(len(T_list)):
                    hi = _hi[0]; _hi[0] += 1
                    spar = hi % 2
                    ssem = "dq2" if spar == 0 else "dq2b"
                    if hi >= 2:
                        rec.wait("sync", "cv", ev_stripv[hi - 2])
                    dma("sync", ssem, q_strip[spar][:, 0:free_T], T_list[hgx])
                    rec.wait("v", ssem, rec.val(ssem))
                    rec.wait("v", "dq3", v_rep)
                    rec.wait("v", "cs", rec.val("cs"))
                    vop(lambda e, p=spar, qr=q_repx, fr=free_T: e.tensor_tensor(q_f[:, 0:fr], q_strip[p][:, 0:fr], qr[:, 0:fr], ALU.mult), inc=True)
                    ev_stripv[hi] = rec.val("cv")
                    fence("v")
                    vop(lambda e, fr=free_T: e.tensor_scalar(q_i8[:, 0:fr], q_f[:, 0:fr], 12582912.0, -12582912.0, ALU.add, ALU.add), inc=True)
                    rec.wait("s", "cv", rec.val("cv"))
                    rec.wait("s", "gwq", rec.val("gwq"))
                    sop(lambda e, fr=free_T: e.copy(out=q_b[:, 0:fr], in_=q_i8[:, 0:fr]), inc=True)
                    rec.wait("g", "cs", rec.val("cs"))
                    dst, srcv = dst_scatter(hgx, q_b)
                    dma("g", "gwq", dst, srcv)
                    yield

            def gen_a2():
                yield from qw_scales(d_wg_nat, ICH, t_sg, t_sinv)
                qw_row(t_sinv, 0, ICH, I)
                yield from qw_scales(d_wu_nat, ICH, t_su, t_sinv2)
                qw_row(t_sinv2, 1, ICH, I)
                yield from qw_scales_wide([d_wd_nat[i] for i in range(DCH)], t_sd, t_sinv3)
                qw_row(t_sinv3, 2, DCH, D)
                yield from qw_strips([d_wgT[i] for i in range(DCH)], 0, I,
                          lambda hgx, qb: (s_wqg[:, hgx, :, :].rearrange("c p e -> p c e"),
                                           qb[:, 0:I].rearrange("p (c e) -> p c e", e=128)), q_rep)
                yield from qw_strips([d_wuT[i] for i in range(DCH)], 1, I,
                          lambda hgx, qb: (s_wqu[:, hgx, :, :].rearrange("c p e -> p c e"),
                                           qb[:, 0:I].rearrange("p (c e) -> p c e", e=128)), q_rep2)
                yield from qw_strips([d_wdT[i] for i in range(ICH)], 2, D,
                          lambda icx, qb: (s_wqd[icx], qb[:, 0:D]), q_rep3)

                for hgx in range(DCH):
                    rec.wait("g", "cv", rec.val("cv"))
                    rec.wait("g", "cs", rec.val("cs"))
                    dma("g", "grow", s_rows[3, hgx * 128:(hgx + 1) * 128].unsqueeze(1), t_sd[:, hgx:hgx + 1])

            _g1, _g2 = gen_ln(), gen_a2()
            _d1 = _d2 = False
            while not (_d1 and _d2):
                if not _d1:
                    try:
                        next(_g1)
                    except StopIteration:
                        _d1 = True
                for _ in range(8):
                    if not _d2:
                        try:
                            next(_g2)
                        except StopIteration:
                            _d2 = True

            rec.wait("sync", "cp", v_cc1done[0])
            dma("sync", "dsc", t_scal[:, 1:2], cc1_out)
            dma("sync", "dsc", t_scal[:, 2:3], d_clipx)
            dma("sync", "dsc", t_scal[:, 3:4], d_clipd)
            v_cc1in = rec.val("dsc")
            rec.wait("s", "dsc", v_cc1in)
            sop(lambda e: e.activation(out=t_scal[:, 4:5], in_=t_scal[:, 2:3], func=ACT.Sigmoid), inc=True)
            rec.wait("v", "cs", rec.val("cs"))
            vop(lambda e: e.tensor_tensor(t_scal[:, 5:6], t_scal[:, 4:5], t_scal[:, 1:2], ALU.mult))
            fence("v")
            vop(lambda e: e.tensor_single_scalar(t_scal[:, 6:7], t_scal[:, 5:6], INV127, ALU.mult))
            fence("v")
            vop(lambda e: e.reciprocal(t_scal[:, 7:8], t_scal[:, 6:7]), inc=True)
            rec.wait("g", "cv", rec.val("cv"))
            rec.wait("g", "grow", rec.val("grow"))
            dma("g", "grow", s_rows[4, 0:2].unsqueeze(0), t_scal[:, 6:8])
            rec.wait("sync", "grow", rec.val("grow"))
            dma("sync", "dsc", t_sxcol[:], s_rows[4:5, 0:1].broadcast_to((128, 1)))
            dma("sync", "dsc", t_invx[:], s_rows[4:5, 1:2].broadcast_to((128, 1)))
            v_invx = rec.val("dsc")
            rec.wait("v", "dsc", v_invx)
            vop(lambda e: e.tensor_scalar(t_sg2[:], t_sg[:], t_sxcol[:], None, ALU.mult))
            vop(lambda e: e.tensor_scalar(t_su2[:], t_su[:], t_sxcol[:], None, ALU.mult), inc=True)

            barrier()

        with contextlib.ExitStack() as esB:
            b_xnb = [mksb(esB, f"b_xnb{i}", [128, DCH, TC + 1]) for i in range(2)]
            b_xm = [mksb(esB, f"b_xm{i}", [128, DCH, TC]) for i in range(2)]
            b_w = [mksb(esB, f"b_w{i}", [128, DCH, 128]) for i in range(2)]
            b_r = [mksb(esB, f"b_r{i}", [128, DCH, TC]) for i in range(2)]
            b_ek = [mksb(esB, f"b_ek{i}", [128, DCH, TC]) for i in range(2)]
            b_v = [mksb(esB, f"b_v{i}", [128, DCH, TC]) for i in range(2)]
            b_td = [mksb(esB, f"b_td{i}", [128, DCH, TC]) for i in range(2)]
            b_u = mksb(esB, "b_u", [128, DCH, TC])
            b_a = mksb(esB, "b_a", [128, DCH, TC])
            b_b = mksb(esB, "b_b", [128, DCH, TC])
            b_acar = mksb(esB, "b_acar", [128, DCH])
            b_bcar = mksb(esB, "b_bcar", [128, DCH])
            b_num = mksb(esB, "b_num", [128, DCH, TC])
            b_den = mksb(esB, "b_den", [128, DCH, TC])
            b_rec = mksb(esB, "b_rec", [128, DCH, TC])
            b_y = [mksb(esB, f"b_y{i}", [128, TC, DCH]) for i in range(2)]
            b_nh = mksb(esB, "b_nh", [128, TC, DCH])
            b_w2 = mksb(esB, "b_w2", [128, DCH])
            b_spk = [mksb(esB, f"b_spk{i}", [128, DCH, TC], BF16) for i in range(2)]

            ev_xm = {}
            ev_group_mm = {}
            ev_evac = {}
            ev_spkdma = {}
            ev_ydone = {}

            def emit_xm(c):
                par = c % 2
                xnsem = "dxn0" if par == 0 else "dxn1"
                if c >= 2:
                    rec.wait("sync", "cv", ev_xm[c - 2])
                if c > 0:
                    dma("sync", xnsem, b_xnb[par][:],
                        s_xn[:, :, c * TC - 1:(c + 1) * TC].rearrange("g p t -> p g t"))
                else:
                    dma("sync", xnsem, b_xnb[0][:, :, 1:TC + 1],
                        s_xn[:, :, 0:TC].rearrange("g p t -> p g t"))
                    vop(lambda e: e.memset(b_xnb[0][:, :, 0:1], 0.0))
                    fence("v")
                v_xnb = rec.val(xnsem)
                rec.wait("v", xnsem, v_xnb)
                if c >= 2:
                    rec.wait("v", "ct", ev_group_mm[(c - 2, 3, DCH - 1)])
                for g in range(DCH):
                    vop(lambda e, g=g, p=par: e.tensor_scalar(
                        b_xm[p][:, g, :], b_xnb[p][:, g, 1:TC + 1], t_mix[:, g:g + 1], None, ALU.mult))
                fence("v")
                for g in range(DCH):
                    vop(lambda e, g=g, p=par: e.scalar_tensor_tensor(
                        b_xm[p][:, g, :], b_xnb[p][:, g, 0:TC], t_imix[:, g:g + 1],
                        b_xm[p][:, g, :], ALU.mult, ALU.add),
                        inc=(g == DCH - 1))
                ev_xm[c] = rec.val("cv")

            def emit_memscan(c):
                par = c % 2
                spsem = "gsp0" if par == 0 else "gsp1"
                if c >= 2:
                    rec.wait("v", spsem, ev_spkdma[c - 2])
                for t in range(TC):
                    if c == 0 and t == 0:
                        src = t_zero8[:]
                    else:
                        src = b_nh[:, t - 1, :] if t > 0 else b_nh[:, TC - 1, :]

                    def f1(e, t=t, s=src, p=par):
                        return e.scalar_tensor_tensor(
                            b_w2[:], s, BETA_, b_y[p][:, t, :], ALU.mult, ALU.add)

                    def f2(e, t=t, s=src):
                        return e.scalar_tensor_tensor(
                            b_nh[:, t, :], s, -1.0, b_w2[:], ALU.is_lt, ALU.add)
                    vop(f1)
                    nfence("v")
                    vop(f2)
                    nfence("v")
                vop(lambda e, p=par: e.tensor_single_scalar(
                    b_spk[p][:], b_nh[:].rearrange("p t g -> p g t"), -1.0, ALU.is_lt), inc=True)
                v_spk = rec.val("cv")
                ev_ydone[c] = v_spk
                rec.wait("g", "cv", v_spk)
                dma("g", spsem, s_spk[:, :, c * TC:(c + 1) * TC].rearrange("g p t -> p g t"), b_spk[par][:])
                ev_spkdma[c] = rec.val(spsem)

            emit_xm(0)
            for c in range(NCH):
                par = c % 2
                if c + 1 < NCH:
                    emit_xm(c + 1)

                for eg in range(DCH):
                    for pj in range(4):
                        k = eg * 4 + pj
                        wpar = k % 2
                        wsem = "dw0" if wpar == 0 else "dw1"
                        gk = c * 32 + k
                        if gk >= 2:
                            pg = gk - 2
                            pc_, pk_ = pg // 32, pg % 32
                            rec.wait("sync", "ct", ev_group_mm[(pc_, pk_ % 4, pk_ // 4)])
                        dma("sync", wsem, b_w[wpar][:], d_wstack[pj, eg])
                        v_w = rec.val(wsem)
                        pbs = pp[pj + 4 * (eg % 2)][:, 0:TC]
                        rec.wait("t", wsem, v_w)
                        rec.wait("t", "cv", ev_xm[c])
                        pkey = (c, pj, eg - 2) if eg > 1 else ((c - 1, pj, eg + DCH - 2) if c > 0 else None)
                        if pkey is not None:
                            eng_s, vv = ev_evac[pkey]
                            rec.wait("t", eng_s, vv)
                        for dg in range(DCH):
                            top(lambda e, pbs=pbs, dg=dg, wp=wpar, xp=par, st=(dg == 0), sp=(dg == DCH - 1):
                                e.matmul(pbs, b_w[wp][:, dg, :], b_xm[xp][:, dg, :],
                                         start=st, stop=sp),
                                inc=(dg == DCH - 1))
                        ev_group_mm[(c, pj, eg)] = rec.val("ct")
                        v_mm = rec.val("ct")
                        v_prevchain = ev_ydone.get(c - 2, 0)
                        if pj == 0:
                            rec.wait("s", "ct", v_mm)
                            rec.wait("s", "cv", v_prevchain)
                            sop(lambda e, eg=eg, p=par, pbs=pbs: e.activation(out=b_r[p][:, eg, :], in_=pbs, func=ACT.Sigmoid), inc=True)
                            ev_evac[(c, pj, eg)] = ("cs", rec.val("cs"))
                        elif pj == 1:
                            rec.wait("s", "ct", v_mm)
                            rec.wait("s", "cv", v_prevchain)
                            sop(lambda e, eg=eg, p=par, pbs=pbs: e.activation(out=b_ek[p][:, eg, :], in_=pbs, func=ACT.Exp), inc=True)
                            ev_evac[(c, pj, eg)] = ("cs", rec.val("cs"))
                        elif pj == 2:
                            rec.wait("s", "ct", v_mm)
                            rec.wait("s", "cv", v_prevchain)
                            sop(lambda e, eg=eg, p=par, pbs=pbs: e.copy(out=b_v[p][:, eg, :], in_=pbs), inc=True)
                            ev_evac[(c, pj, eg)] = ("cs", rec.val("cs"))
                        else:
                            rec.wait("s", "ct", v_mm)
                            rec.wait("s", "cv", v_prevchain)
                            sop(lambda e, eg=eg, p=par, pbs=pbs: e.activation(out=b_td[p][:, eg, :], in_=pbs, func=ACT.Exp), inc=True)
                            ev_evac[(c, pj, eg)] = ("cs", rec.val("cs"))
                    g = eg
                    for pjx in range(4):
                        eng_s, vv = ev_evac[(c, pjx, g)]
                        rec.wait("v", eng_s, vv)
                    if c == 0 and g == 0:
                        rec.wait("v", "cs", ev_gd)
                    ia = t_zero8[:, g:g + 1] if c == 0 else b_acar[:, g:g + 1]
                    ib = t_zero8[:, g:g + 1] if c == 0 else b_bcar[:, g:g + 1]
                    vop(lambda e, g=g, p=par: e.tensor_single_scalar(
                        b_ek[p][:, g, :], b_ek[p][:, g, :], EXP10, ALU.min))
                    fence("v")
                    vop(lambda e, g=g, p=par: e.tensor_tensor(
                        b_u[:, g, :], b_ek[p][:, g, :], b_v[p][:, g, :], ALU.mult))
                    fence("v")
                    vop(lambda e, g=g, p=par, ia=ia: e.tensor_tensor_scan(
                        b_a[:, g, :], b_td[p][:, g, :], b_u[:, g, :], ia, ALU.mult, ALU.add))
                    vop(lambda e, g=g, p=par, ib=ib: e.tensor_tensor_scan(
                        b_b[:, g, :], b_td[p][:, g, :], b_ek[p][:, g, :], ib, ALU.mult, ALU.add))
                    fence("v")
                    vop(lambda e, g=g: e.tensor_copy(b_acar[:, g:g + 1], b_a[:, g, TC - 1:TC]))
                    vop(lambda e, g=g: e.tensor_copy(b_bcar[:, g:g + 1], b_b[:, g, TC - 1:TC]))
                    vop(lambda e, g=g: e.scalar_tensor_tensor(
                        b_num[:, g, :], b_u[:, g, :], t_gd[:, g:g + 1], b_a[:, g, :], ALU.mult, ALU.add))
                    vop(lambda e, g=g, p=par: e.scalar_tensor_tensor(
                        b_den[:, g, :], b_ek[p][:, g, :], t_gd[:, g:g + 1], b_b[:, g, :], ALU.mult, ALU.add))
                    fence("v")
                    vop(lambda e, g=g: e.tensor_single_scalar(b_rec[:, g, :], b_den[:, g, :], EPS, ALU.add))
                    fence("v")
                    vop(lambda e, g=g: e.reciprocal(b_den[:, g, :], b_rec[:, g, :]))
                    fence("v")
                    vop(lambda e, g=g: e.scalar_tensor_tensor(
                        b_rec[:, g, :], b_num[:, g, :], -5.0, b_den[:, g, :], ALU.mult, ALU.mult))
                    fence("v")
                    vop(lambda e, g=g, p=par: e.tensor_tensor(
                        b_y[p][:, :, g], b_rec[:, g, :], b_r[p][:, g, :], ALU.mult),
                        inc=(g == DCH - 1))
                    fence("v")

                emit_memscan(c)

            barrier()

        with contextlib.ExitStack() as esC:
            esC1 = contextlib.ExitStack()
            c_xn = [mksb(esC1, f"c_xn{i}", [128, DCH, FTC]) for i in range(2)]
            c_xq8 = mksb(esC1, "c_xq8", [128, DCH, FTC], I8)
            c_xf = mksb(esC1, "c_xf", [128, DCH, FTC])
            c_xqb = [mksb(esC1, f"c_xqb{i}", [128, DCH, FTC], BF16) for i in range(2)]
            c_wg = [mksb(esC1, f"c_wg{i}", [128, DCH, 128], BF16) for i in range(2)]
            c_wu = [mksb(esC1, f"c_wu{i}", [128, DCH, 128], BF16) for i in range(2)]
            c_sg = mksb(esC1, "c_sg", [128, FTC])
            c_su = mksb(esC1, "c_su", [128, FTC])
            c_h = [mksb(esC1, f"c_h{i}", [128, FTC]) for i in range(2)]

            ev_xq = {}
            ev_mm1 = {}
            ev_evac1 = {}
            ev_h = {}
            ev_hdma = {}
            def emit_xq(fc):
                par = fc % 2
                cxsem = "dcx0" if par == 0 else "dcx1"
                if fc >= 2:
                    rec.wait("sync", "cs", ev_xq[fc - 2])
                dma("sync", cxsem, c_xn[par][:], s_xn[:, :, fc * FTC:(fc + 1) * FTC].rearrange("g p t -> p g t"))
                v_cxn = rec.val(cxsem)
                if fc >= 2:
                    rec.wait("s", "ct", ev_mm1[(fc - 2, ICH - 1)])
                rec.wait("v", cxsem, v_cxn)
                rec.wait("v", "dsc", v_invx)
                if fc >= 2:
                    rec.wait("v", "ct", ev_mm1[(fc - 2, ICH - 1)])
                if fc >= 1:
                    rec.wait("v", "cs", ev_xq[fc - 1])
                vop(lambda e, p=par: e.tensor_scalar(c_xf[:], c_xn[p][:], t_invx[:], 12582912.0, ALU.mult, ALU.add))
                fence("v")
                vop(lambda e: e.tensor_scalar(c_xf[:], c_xf[:], -12582912.0, -128.0, ALU.add, ALU.max))
                fence("v")
                vop(lambda e: e.tensor_single_scalar(c_xq8[:], c_xf[:], 127.0, ALU.min), inc=True)
                rec.wait("s", "cv", rec.val("cv"))
                sop(lambda e, p=par: e.copy(out=c_xqb[p][:], in_=c_xq8[:]), inc=True)
                ev_xq[fc] = rec.val("cs")

            emit_xq(0)
            for fc in range(NFC):
                par = fc % 2
                v_xq = ev_xq[fc]
                if fc + 1 < NFC:
                    emit_xq(fc + 1)

                for ic in range(ICH):
                    kk = fc * ICH + ic
                    wpar = kk % 2
                    wgsem = "dwg0" if wpar == 0 else "dwg1"
                    if kk >= 2:
                        pk = kk - 2
                        rec.wait("sync", "ct", ev_mm1[(pk // ICH, pk % ICH)])
                    dma("sync", wgsem, c_wg[wpar][:], s_wqg[ic].rearrange("d p e -> p d e"))
                    dma("sync", wgsem, c_wu[wpar][:], s_wqu[ic].rearrange("d p e -> p d e"))
                    v_wq = rec.val(wgsem)
                    pslot = kk % 2
                    pgt = p_c[0] if pslot == 0 else p_c[2]
                    put = p_c[1] if pslot == 0 else p_c[3]
                    rec.wait("t", wgsem, v_wq)
                    rec.wait("t", "cs", v_xq)
                    if kk >= 2:
                        pk = kk - 2
                        rec.wait("t", "cs", ev_evac1[(pk // ICH, pk % ICH)])
                    for dg in range(DCH):
                        top(lambda e, pgt=pgt, dg=dg, wp=wpar, xp=par, st=(dg == 0), sp=(dg == DCH - 1):
                            e.matmul(pgt[:], c_wg[wp][:, dg, :], c_xqb[xp][:, dg, :], start=st, stop=sp))
                    for dg in range(DCH):
                        top(lambda e, put=put, dg=dg, wp=wpar, xp=par, st=(dg == 0), sp=(dg == DCH - 1):
                            e.matmul(put[:], c_wu[wp][:, dg, :], c_xqb[xp][:, dg, :], start=st, stop=sp),
                            inc=(dg == DCH - 1))
                    ev_mm1[(fc, ic)] = rec.val("ct")
                    v_mm2 = rec.val("ct")
                    rec.wait("s", "ct", v_mm2)
                    if kk >= 1:
                        pk = kk - 1
                        rec.wait("s", "cv", ev_h[(pk // ICH, pk % ICH)])
                    sop(lambda e, ic=ic, pgt=pgt: e.activation(out=c_sg[:], in_=pgt[:], func=ACT.Silu, scale=t_sg2[:, ic:ic + 1]))
                    sop(lambda e, ic=ic, put=put: e.activation(out=c_su[:], in_=put[:], func=ACT.Copy, scale=t_su2[:, ic:ic + 1]), inc=True)
                    ev_evac1[(fc, ic)] = rec.val("cs")
                    v_ev = rec.val("cs")
                    hpar = kk % 2
                    hsem = "gh0" if hpar == 0 else "gh1"
                    rec.wait("v", "cs", v_ev)
                    if kk >= 2:
                        rec.wait("v", hsem, ev_hdma[kk - 2])
                    vop(lambda e, hp=hpar: e.tensor_tensor(c_h[hp][:], c_sg[:], c_su[:], ALU.mult), inc=True)
                    ev_h[(fc, ic)] = rec.val("cv")
                    fence("v")
                    vop(lambda e, hp=hpar: e.tensor_reduce(t_tmpc[:], c_h[hp][:], AXX, ALU.max, apply_absolute_value=True))
                    fence("v")
                    vop(lambda e: e.tensor_tensor(t_absh[:], t_absh[:], t_tmpc[:], ALU.max), inc=True)
                    rec.wait("g", "cv", rec.val("cv"))
                    j0 = fc * (FTC // 128)
                    dma("g", hsem, s_h[j0:j0 + 4, :, ic, :].rearrange("j p t -> p j t"),
                        c_h[hpar][:].rearrange("p (j t) -> p j t", t=128))
                    ev_hdma[kk] = rec.val(hsem)

            barrier()
            esC1.close()
            c_hre = [mksb(esC, f"c_hre{i}", [128, ICH, 128]) for i in range(2)]
            c_hq8 = mksb(esC, "c_hq8", [128, ICH, 128], I8)
            c_hf = mksb(esC, "c_hf", [128, ICH, 128])
            c_hqb = [mksb(esC, f"c_hqb{i}", [128, ICH, 128], BF16) for i in range(2)]
            c_wdr = mksb(esC, "c_wdr", [128, ICH, D], BF16)
            c_sdrep = mksb(esC, "c_sdrep", [128, D])
            c_spkT = [mksb(esC, f"c_spkT{i}", [128, D], BF16) for i in range(2)]
            c_spkF = [mksb(esC, f"c_spkF{i}", [128, D]) for i in range(2)]
            c_xres = [mksb(esC, f"c_xres{i}", [128, D]) for i in range(2)]
            c_xs = [mksb(esC, f"c_xs{i}", [128, D]) for i in range(2)]
            c_t1 = mksb(esC, "c_t1", [128, D])
            c_out = [mksb(esC, f"c_out{i}", [128, D]) for i in range(2)]

            for ic in range(ICH):
                dma("sync", "dwd0", c_wdr[:, ic, :], s_wqd[ic])
            v_wdres = rec.val("dwd0")

            fence("v")
            vop(lambda e: e.tensor_copy(t_tmpc[:], t_absh[:]), inc=True)
            rec.wait("t", "cv", rec.val("cv"))
            rec.wait("t", "cs", rec.val("cs"))
            top(lambda e: e.transpose(p_tp[0][0:1, :], t_tmpc[:], t_id[:]), inc=True)
            rec.wait("s", "ct", rec.val("ct"))
            sop(lambda e: e.copy(out=t_rowred[:], in_=p_tp[0][0:1, :]), inc=True)
            rec.wait("v", "cs", rec.val("cs"))
            vop(lambda e: e.tensor_reduce(t_scal[:, 8:9], t_rowred[:], AXX, ALU.max), inc=True)
            rec.wait("g", "cv", rec.val("cv"))
            dma("g", "gcc", cc2_in, t_scal[:, 8:9])
            rec.wait("g", "gcc", rec.val("gcc"))

            def f_cc2(e):
                return e.collective_compute(
                    "AllReduce", ALU.max,
                    replica_groups=[list(range(NCORES))],
                    ins=[cc2_in.opt()], outs=[cc2_out.opt()])
            rec.op("g", f_cc2, inc="cp")
            rec.wait("sync", "cp", rec.val("cp"))
            dma("sync", "dsc", t_scal[:, 9:10], cc2_out)
            v_cc2 = rec.val("dsc")
            rec.wait("s", "dsc", v_cc2)
            sop(lambda e: e.activation(out=t_scal[:, 10:11], in_=t_scal[:, 3:4], func=ACT.Sigmoid), inc=True)
            rec.wait("v", "cs", rec.val("cs"))
            vop(lambda e: e.tensor_tensor(t_scal[:, 11:12], t_scal[:, 10:11], t_scal[:, 9:10], ALU.mult))
            fence("v")
            vop(lambda e: e.tensor_single_scalar(t_scal[:, 12:13], t_scal[:, 11:12], INV127, ALU.mult))
            fence("v")
            vop(lambda e: e.reciprocal(t_scal[:, 13:14], t_scal[:, 12:13]), inc=True)
            rec.wait("g", "cv", rec.val("cv"))
            rec.wait("g", "gcc", rec.val("gcc"))
            dma("g", "gcc", s_rows[4, 2:4].unsqueeze(0), t_scal[:, 12:14])
            rec.wait("sync", "gcc", rec.val("gcc"))
            dma("sync", "dsc", t_shcol[:], s_rows[4:5, 2:3].broadcast_to((128, 1)))
            dma("sync", "dsc", t_invh[:], s_rows[4:5, 3:4].broadcast_to((128, 1)))
            dma("sync", "dsc", c_sdrep[:], s_rows[3:4, 0:D].broadcast_to((128, D)))
            v_sdr = rec.val("dsc")
            v_invh = rec.val("dsc")
            rec.wait("v", "dsc", v_sdr)
            vop(lambda e: e.tensor_scalar(c_sdrep[:], c_sdrep[:], t_shcol[:], None, ALU.mult), inc=True)
            v_sdrep = rec.val("cv")

            rec.wait("sync", "gh0", rec.val("gh0"))
            rec.wait("sync", "gh1", rec.val("gh1"))
            rec.wait("sync", "gsp0", rec.val("gsp0"))
            rec.wait("sync", "gsp1", rec.val("gsp1"))
            ev_hq = {}
            ev_spkf = {}
            ev_out = {}
            ev_mm2w = {}
            ev_devac = {}
            ev_odma = {}
            ev_xs = {}

            def emit_prep(j):
                par = j % 2
                hrsem = "dhr0" if par == 0 else "dhr1"
                spsem2 = "dsp0" if par == 0 else "dsp1"
                if j >= 2:
                    rec.wait("sync", "cs", ev_hq[j - 2])
                dma("sync", hrsem, c_hre[par][:], s_h[j])
                v_hre = rec.val(hrsem)
                if j >= 2:
                    rec.wait("sync", "cs", ev_spkf[j - 2])
                    rec.wait("sync", "cv", ev_out[j - 2])
                for g in range(DCH):
                    dma("sync", spsem2,
                        c_spkT[par][:, g * 128:(g + 1) * 128],
                        s_spk[g, :, j * 128:(j + 1) * 128],
                        transpose=True)
                dma("sync", spsem2, c_xres[par][:], d_x[j * 128:(j + 1) * 128, :])
                v_in2 = rec.val(spsem2)
                if j >= 2:
                    rec.wait("s", "ct", ev_mm2w[(j - 2, ICH - 1)])
                rec.wait("v", hrsem, v_hre)
                rec.wait("v", "dsc", v_invh)
                if j >= 2:
                    rec.wait("v", "ct", ev_mm2w[(j - 2, ICH - 1)])
                if j >= 1:
                    rec.wait("v", "cs", ev_hq[j - 1])
                vop(lambda e, p=par: e.tensor_scalar(c_hf[:], c_hre[p][:], t_invh[:], 12582912.0, ALU.mult, ALU.add))
                fence("v")
                vop(lambda e: e.tensor_scalar(c_hf[:], c_hf[:], -12582912.0, -128.0, ALU.add, ALU.max))
                fence("v")
                vop(lambda e: e.tensor_single_scalar(c_hq8[:], c_hf[:], 127.0, ALU.min), inc=True)
                rec.wait("s", "cv", rec.val("cv"))
                sop(lambda e, p=par: e.copy(out=c_hqb[p][:], in_=c_hq8[:]), inc=True)
                ev_hq[j] = rec.val("cs")
                rec.wait("s", spsem2, v_in2)
                sop(lambda e, p=par: e.copy(out=c_spkF[p][:], in_=c_spkT[p][:]), inc=True)
                ev_spkf[j] = rec.val("cs")
                v_spkf = rec.val("cs")
                rec.wait("v", "cs", v_spkf)
                rec.wait("v", spsem2, v_in2)
                if j >= 2:
                    rec.wait("v", "cv", ev_out[j - 2])
                vop(lambda e, p=par: e.tensor_tensor(c_xs[p][:], c_xres[p][:], c_spkF[p][:], ALU.add), inc=True)
                ev_xs[j] = rec.val("cv")

            emit_prep(0)
            for j in range(TTILES):
                par = j % 2
                osem = "go0" if par == 0 else "go1"
                v_hq = ev_hq[j]
                v_xs = ev_xs[j]
                if j + 1 < TTILES:
                    emit_prep(j + 1)

                pdA = p_c[2] if par == 0 else pp[4]
                pdB = p_c[3] if par == 0 else pp[5]
                rec.wait("t", "dwd0", v_wdres)
                rec.wait("t", "cs", v_hq)
                if j >= 2:
                    rec.wait("t", "cv", ev_devac[j - 2])
                for ic in range(ICH):
                    top(lambda e, pdA=pdA, ic=ic, hp=par, st=(ic == 0), sp=(ic == ICH - 1):
                        e.matmul(pdA[:], c_hqb[hp][:, ic, :], c_wdr[:, ic, 0:512],
                                 start=st, stop=sp, skip_group_check=True))
                    top(lambda e, pdB=pdB, ic=ic, hp=par, st=(ic == 0), sp=(ic == ICH - 1):
                        e.matmul(pdB[:], c_hqb[hp][:, ic, :], c_wdr[:, ic, 512:D],
                                 start=st, stop=sp, skip_group_check=True),
                        inc=True)
                    ev_mm2w[(j, ic)] = rec.val("ct")
                v_dmm = rec.val("ct")
                rec.wait("v", "ct", v_dmm)
                rec.wait("v", "cv", max(v_sdrep, v_xs))
                fence("v")
                vop(lambda e, pdA=pdA: e.tensor_tensor(c_t1[:, 0:512], pdA[:], c_sdrep[:, 0:512], ALU.mult))
                vop(lambda e, pdB=pdB: e.tensor_tensor(c_t1[:, 512:D], pdB[:], c_sdrep[:, 512:D], ALU.mult), inc=True)
                ev_devac[j] = rec.val("cv")
                fence("v")
                if j >= 2:
                    rec.wait("v", osem, ev_odma[j - 2])
                vop(lambda e, p=par: e.tensor_tensor(c_out[p][:], c_t1[:], c_xs[p][:], ALU.add), inc=True)
                ev_out[j] = rec.val("cv")
                rec.wait("g", "cv", ev_out[j])
                dma("g", osem, d_out[j * 128:(j + 1) * 128, :], c_out[par][:])
                ev_odma[j] = rec.val(osem)

            rec.wait("g", "go0", rec.val("go0"))
            rec.wait("g", "go1", rec.val("go1"))

        @block.sync
        def _(sync):
            rec.replay("sync", sync, handles)

        @block.vector
        def _(vector):
            rec.replay("v", vector, handles)

        @block.scalar
        def _(scalar):
            rec.replay("s", scalar, handles)

        @block.tensor
        def _(tensor):
            rec.replay("t", tensor, handles)

        @block.gpsimd
        def _(gpsimd):
            rec.replay("g", gpsimd, handles)

    return nc


def kernel(L0_attn_embeddings, L0_attn_W_R, L0_attn_W_K, L0_attn_W_V, L0_attn_W_td,
           L0_attn_global_decay, L0_attn_mix, L0_ffn_clip_x, L0_ffn_clip_down,
           L0_ffn_w_gate, L0_ffn_w_up, L0_ffn_w_down):
    x = np.ascontiguousarray(np.asarray(L0_attn_embeddings, dtype=np.float32))

    def wtiles(w):
        w = np.asarray(w, dtype=np.float32)
        return np.ascontiguousarray(w.reshape(DCH, 128, DCH, 128).transpose(0, 3, 2, 1))

    wstack = np.ascontiguousarray(np.stack([
        wtiles(L0_attn_W_R), wtiles(L0_attn_W_K),
        wtiles(L0_attn_W_V), wtiles(L0_attn_W_td)], axis=0))
    gd_col = np.ascontiguousarray(np.asarray(L0_attn_global_decay, np.float32).reshape(DCH, 128).T)
    mix_col = np.ascontiguousarray(np.asarray(L0_attn_mix, np.float32).reshape(DCH, 128).T)
    wg = np.asarray(L0_ffn_w_gate, np.float32)
    wu = np.asarray(L0_ffn_w_up, np.float32)
    wd = np.asarray(L0_ffn_w_down, np.float32)
    wg_nat = np.ascontiguousarray(wg.reshape(ICH, 128, D))
    wu_nat = np.ascontiguousarray(wu.reshape(ICH, 128, D))
    wd_nat = np.ascontiguousarray(wd.reshape(DCH, 128, I))
    wgT = np.ascontiguousarray(wg.T).reshape(DCH, 128, I)
    wuT = np.ascontiguousarray(wu.T).reshape(DCH, 128, I)
    wdT = np.ascontiguousarray(wd.T).reshape(ICH, 128, D)
    clipx = np.asarray(L0_ffn_clip_x, np.float32).reshape(1, 1)
    clipd = np.asarray(L0_ffn_clip_down, np.float32).reshape(1, 1)
    idm = np.eye(128, dtype=np.float32)

    nc = build_program()
    in_maps = []
    for b in range(NCORES):
        in_maps.append({
            "x": np.ascontiguousarray(x[b]),
            "wstack": wstack, "gdraw": gd_col, "mixv": mix_col,
            "clipx": clipx, "clipd": clipd,
            "wg_nat": wg_nat, "wu_nat": wu_nat, "wd_nat": wd_nat,
            "wgT": wgT, "wuT": wuT, "wdT": wdT, "idm": idm,
        })
    res = run_bass_kernel_spmd(nc, in_maps, core_ids=list(range(NCORES)))
    out = np.stack([np.asarray(res.results[b]["out"]) for b in range(NCORES)], axis=0)
    return out


# version 19 (final)
# speedup vs baseline: 1.0201x; 1.0201x over previous
import contextlib
import os
import numpy as np
import ml_dtypes
import concourse.bass as bass
import concourse.mybir as mybir
from concourse.bass_utils import run_bass_kernel_spmd

if os.environ.get("BASS_SIM_FP64") == "1":
    import concourse.bass_interp as _bi
    if not getattr(_bi.MultiCoreSim, "_nonfinite_ok_patch", False):
        _orig_mcs_init = _bi.MultiCoreSim.__init__

        def _mcs_init(self, *a, **kw):
            kw.setdefault("require_finite", False)
            kw.setdefault("require_nnan", False)
            _orig_mcs_init(self, *a, **kw)
        _bi.MultiCoreSim.__init__ = _mcs_init
        _bi.MultiCoreSim._nonfinite_ok_patch = True

F32 = mybir.dt.float32
BF16 = mybir.dt.bfloat16
I8 = mybir.dt.int8
ALU = mybir.AluOpType
ACT = mybir.ActivationFunctionType
AXX = mybir.AxisListType.X

B, T, D, I = 8, 4096, 1024, 2816
NCORES = 8
DCH = D // 128
ICH = I // 128
TTILES = T // 128
TC = 256
NCH = T // TC
FTC = 512
NFC = T // FTC
EPS = 1e-6
EPS_LN = 1e-5
BETA_ = 0.3
EXP10 = float(np.exp(np.float32(10.0)).astype(np.float32))
INV127 = float(np.float32(1.0) / np.float32(127.0))
INVD = float(np.float32(1.0) / np.float32(D))

SEMS = ["cv", "cs", "ct", "cp",
        "dxa", "dxb", "dq1", "dq1b", "dq2", "dq2b", "dq2c", "dq2d", "dq3", "dsc",
        "dxn0", "dxn1", "dw0", "dw1",
        "dcx0", "dcx1", "dwg0", "dwg1",
        "dhr0", "dhr1", "dsp0", "dsp1", "dwd0", "dwd1",
        "gacc", "grow", "gwq", "gcc",
        "gsp0", "gsp1", "gh0", "gh1", "go0", "go1"]


class Rec:
    def __init__(self):
        self.ops = []
        self.cnt = {s: 0 for s in SEMS}

    def op(self, eng, fn, inc=None, amt=1):
        if inc is not None:
            self.cnt[inc] += amt
            self.ops.append((eng, fn, inc, amt))
        else:
            self.ops.append((eng, fn, None, 0))

    def val(self, name):
        return self.cnt[name]

    def wait(self, eng, name, v=None):
        if v is None:
            v = self.cnt[name]
        assert v <= self.cnt[name], (name, v, self.cnt[name])
        if v <= 0:
            return

        def f(e, _n=name, _v=v):
            e.wait_ge(f.handles[_n], _v)
        self.ops.append((eng, f, None, 0))

    def replay(self, eng_key, engine, handles):
        for (k, fn, inc, amt) in self.ops:
            if k != eng_key:
                continue
            fn.handles = handles
            inst = fn(engine)
            if inc is not None:
                inst.then_inc(handles[inc], amt)


_WITH_FENCES = os.environ.get("BASS_SIM_FP64") == "1"


def build_program():
    nc = bass.Bass("TRN2", target_bir_lowering=False)

    d_x = nc.dram_tensor("x", [T, D], F32, kind="ExternalInput").ap()
    d_wstack = nc.dram_tensor("wstack", [4, DCH, 128, DCH, 128], F32, kind="ExternalInput").ap()
    d_gd = nc.dram_tensor("gdraw", [128, DCH], F32, kind="ExternalInput").ap()
    d_mix = nc.dram_tensor("mixv", [128, DCH], F32, kind="ExternalInput").ap()
    d_clipx = nc.dram_tensor("clipx", [1, 1], F32, kind="ExternalInput").ap()
    d_clipd = nc.dram_tensor("clipd", [1, 1], F32, kind="ExternalInput").ap()
    d_wg_nat = nc.dram_tensor("wg_nat", [ICH, 128, D], F32, kind="ExternalInput").ap()
    d_wu_nat = nc.dram_tensor("wu_nat", [ICH, 128, D], F32, kind="ExternalInput").ap()
    d_wd_nat = nc.dram_tensor("wd_nat", [DCH, 128, I], F32, kind="ExternalInput").ap()
    d_wgT = nc.dram_tensor("wgT", [DCH, 128, I], F32, kind="ExternalInput").ap()
    d_wuT = nc.dram_tensor("wuT", [DCH, 128, I], F32, kind="ExternalInput").ap()
    d_wdT = nc.dram_tensor("wdT", [ICH, 128, D], F32, kind="ExternalInput").ap()
    d_id = nc.dram_tensor("idm", [128, 128], F32, kind="ExternalInput").ap()
    d_out = nc.dram_tensor("out", [T, D], F32, kind="ExternalOutput").ap()

    s_xn = nc.dram_tensor("xn_s", [DCH, 128, T], F32).ap()
    s_spk = nc.dram_tensor("spk_s", [DCH, 128, T], BF16).ap()
    s_h = nc.dram_tensor("h_s", [TTILES, 128, ICH, 128], F32).ap()
    s_wqg = nc.dram_tensor("wqg_s", [ICH, DCH, 128, 128], BF16).ap()
    s_wqu = nc.dram_tensor("wqu_s", [ICH, DCH, 128, 128], BF16).ap()
    s_wqd = nc.dram_tensor("wqd_s", [ICH, 128, D], BF16).ap()
    s_rows = nc.dram_tensor("rows_s", [8, I], F32).ap()
    cc1_in = nc.dram_tensor("cc1_in", [1, 1], F32).ap()
    cc1_out = nc.dram_tensor("cc1_out", [1, 1], F32).ap()
    cc2_in = nc.dram_tensor("cc2_in", [1, 1], F32).ap()
    cc2_out = nc.dram_tensor("cc2_out", [1, 1], F32).ap()

    rec = Rec()

    es = contextlib.ExitStack()
    with es:
        handles = {}
        for s in SEMS:
            handles[s] = es.enter_context(nc.semaphore(s))

        def mksb(stack, name, shape, dt=F32):
            return stack.enter_context(nc.sbuf_tensor(name, shape, dt))

        t_id = mksb(es, "t_id", [128, 128])
        t_mix = mksb(es, "t_mix", [128, DCH])
        t_imix = mksb(es, "t_imix", [128, DCH])
        t_gd = mksb(es, "t_gd", [128, DCH])
        t_scal = mksb(es, "t_scal", [1, 16])
        t_invx = mksb(es, "t_invx", [128, 1])
        t_invh = mksb(es, "t_invh", [128, 1])
        t_sxcol = mksb(es, "t_sxcol", [128, 1])
        t_shcol = mksb(es, "t_shcol", [128, 1])
        t_absx = mksb(es, "t_absx", [128, 1])
        t_absh = mksb(es, "t_absh", [128, 1])
        t_tmpc = mksb(es, "t_tmpc", [128, 1])
        t_tmp4 = mksb(es, "t_tmp4", [128, 4])
        t_rowred = mksb(es, "t_rowred", [1, 128])
        t_sg = mksb(es, "t_sg", [128, ICH])
        t_su = mksb(es, "t_su", [128, ICH])
        t_sd = mksb(es, "t_sd", [128, DCH])
        t_sg2 = mksb(es, "t_sg2", [128, ICH])
        t_su2 = mksb(es, "t_su2", [128, ICH])
        t_sinv = mksb(es, "t_sinv", [128, ICH])
        t_sinv2 = mksb(es, "t_sinv2", [128, ICH])
        t_sinv3 = mksb(es, "t_sinv3", [128, ICH])
        t_sinvT = mksb(es, "t_sinvT", [ICH, 128])
        t_zero8 = mksb(es, "t_zero8", [128, DCH])
        t_epsln = mksb(es, "t_epsln", [128, 1])
        t_mag = mksb(es, "t_mag", [128, 1])
        t_nmag = mksb(es, "t_nmag", [128, 1])

        pp = [es.enter_context(nc.psum_tensor(f"pp{i}", [128, 512], F32)) for i in range(8)]
        p_c = pp[:4]
        p_b = [pp[i][:, 0:TC] for i in range(4)]
        p_tp = [pp[4 + i][:, 0:128] for i in range(2)]

        block = es.enter_context(nc.Block())

        def dma(eng_q, sem, out_ap, in_ap, transpose=False):
            if transpose:
                def f(e, o=out_ap, i=in_ap):
                    return e.dma_start_transpose(out=o, in_=i)
            else:
                def f(e, o=out_ap, i=in_ap):
                    return e.dma_start(out=o, in_=i)
            rec.op(eng_q, f, inc=sem, amt=16)

        def vop(fn, inc=False):
            rec.op("v", fn, inc=("cv" if inc else None))

        def sop(fn, inc=False):
            rec.op("s", fn, inc=("cs" if inc else None))

        def top(fn, inc=False):
            rec.op("t", fn, inc=("ct" if inc else None))

        def fence(eng):
            rec.op(eng, lambda e: e.drain())

        def nfence(eng):
            if _WITH_FENCES:
                rec.op(eng, lambda e: e.drain())
            else:
                rec.op(eng, lambda e: e.engine_nop())

        def fence_inc(eng, sem):
            rec.op(eng, lambda e: e.drain(), inc=sem)

        def barrier():
            fence_inc("v", "cv")
            fence_inc("s", "cs")
            fence_inc("t", "ct")
            fence_inc("g", "cp")
            tg = [(s, rec.val(s)) for s in SEMS]
            for eng in ["v", "s", "t", "g", "sync"]:
                for (sn, sv) in tg:
                    rec.wait(eng, sn, sv)

        dma("sync", "dsc", t_id[:], d_id)
        dma("sync", "dsc", t_gd[:], d_gd)
        dma("sync", "dsc", t_mix[:], d_mix)
        v0 = rec.val("dsc")
        rec.wait("v", "dsc", v0)
        vop(lambda e: e.tensor_scalar(t_imix[:], t_mix[:], -1.0, 1.0, ALU.mult, ALU.add))
        vop(lambda e: e.memset(t_zero8[:], 0.0))
        vop(lambda e: e.memset(t_absx[:], 0.0))
        vop(lambda e: e.memset(t_absh[:], 0.0))
        vop(lambda e: e.memset(t_epsln[:], EPS_LN))
        vop(lambda e: e.memset(t_mag[:], 12582912.0))
        vop(lambda e: e.memset(t_nmag[:], -12582912.0))
        fence_inc("v", "cv")
        rec.wait("s", "dsc", v0)
        sop(lambda e: e.activation(out=t_gd[:], in_=t_gd[:], func=ACT.Exp), inc=True)
        ev_gd = rec.val("cs")

        with contextlib.ExitStack() as esA:
            a_x = [mksb(esA, f"a_x{i}", [128, 4, D]) for i in range(2)]
            a_sq = mksb(esA, "a_sq", [128, 4, D])
            a_xn = mksb(esA, "a_xn", [128, 4, D])
            a_cm = mksb(esA, "a_cm", [128, 4])
            a_cv = mksb(esA, "a_cv", [128, 4])
            a_sd = mksb(esA, "a_sd", [128, 4])
            a_rs = mksb(esA, "a_rs", [128, 4])
            a_acc = mksb(esA, "a_acc", [128, DCH, 512])
            q_nat = [mksb(esA, f"q_nat{i}", [128, 2, D]) for i in range(2)]
            q_strip = [mksb(esA, f"q_strip{i}", [128, I]) for i in range(4)]
            q_rep = mksb(esA, "q_rep", [128, I])
            q_f = mksb(esA, "q_f", [128, I])
            q_i8 = mksb(esA, "q_i8", [128, I], I8)
            q_b = mksb(esA, "q_b", [128, I], BF16)

            ev_xn_v = {}
            ev_tp = {}
            ev_acc_dma = {}

            def gen_ln():
              for ib in range(TTILES // 4):
                par = ib % 2
                xsem = "dxa" if par == 0 else "dxb"
                if ib >= 2:
                    rec.wait("sync", "cv", ev_xn_v[ib - 2])
                dma("sync", xsem, a_x[par][:],
                    d_x[ib * 512:(ib + 1) * 512, :].rearrange("(s p) d -> p s d", p=128))
                v_in = rec.val(xsem)
                rec.wait("s", xsem, v_in)
                if ib >= 1:
                    rec.wait("s", "cv", rec.val("cv"))
                sop(lambda e, p=par: e.activation(out=a_sq[:], in_=a_x[p][:], func=ACT.Square), inc=True)
                v_sq = rec.val("cs")
                rec.wait("v", xsem, v_in)
                rec.wait("v", "cs", v_sq)
                for i4 in range(4):
                    vop(lambda e, p=par, i4=i4: e.tensor_reduce(a_cm[:, i4:i4 + 1], a_x[p][:, i4, :], AXX, ALU.add))
                    vop(lambda e, i4=i4: e.tensor_reduce(a_cv[:, i4:i4 + 1], a_sq[:, i4, :], AXX, ALU.add))
                fence("v")
                vop(lambda e: e.tensor_single_scalar(a_cm[:], a_cm[:], INVD, ALU.mult))
                vop(lambda e: e.tensor_single_scalar(a_cv[:], a_cv[:], INVD, ALU.mult))
                fence("v")
                vop(lambda e: e.tensor_tensor(a_sd[:], a_cm[:], a_cm[:], ALU.mult))
                fence("v")
                vop(lambda e: e.tensor_tensor(a_cv[:], a_cv[:], a_sd[:], ALU.subtract), inc=True)
                v_var = rec.val("cv")
                rec.wait("s", "cv", v_var)
                sop(lambda e: e.activation(out=a_sd[:], in_=a_cv[:], func=ACT.Sqrt, bias=t_epsln[:]), inc=True)
                rec.wait("v", "cs", rec.val("cs"))
                vop(lambda e: e.reciprocal(a_rs[:], a_sd[:]))
                fence("v")
                if ib >= 1:
                    rec.wait("v", "ct", ev_tp[ib - 1])
                for i4 in range(4):
                    vop(lambda e, p=par, i4=i4: e.tensor_scalar(
                        a_xn[:, i4, :], a_x[p][:, i4, :], a_cm[:, i4:i4 + 1], a_rs[:, i4:i4 + 1],
                        ALU.subtract, ALU.mult), inc=(i4 == 3))
                v_xn = rec.val("cv")
                fence("v")
                vop(lambda e: e.tensor_reduce(t_tmpc[:], a_xn[:].rearrange("p s d -> p (s d)"), AXX, ALU.max, apply_absolute_value=True))
                fence("v")
                vop(lambda e: e.tensor_tensor(t_absx[:], t_absx[:], t_tmpc[:], ALU.max), inc=True)
                ev_xn_v[ib] = rec.val("cv")
                rec.wait("t", "cv", v_xn)
                if ib >= 1:
                    rec.wait("s", "gacc", ev_acc_dma[ib - 1])
                for i4 in range(4):
                    for g in range(DCH):
                        sl = (i4 * DCH + g) % 2
                        rec.wait("t", "cs", rec.val("cs"))
                        top(lambda e, g=g, i4=i4, sl=sl: e.transpose(
                            p_tp[sl], a_xn[:, i4, g * 128:(g + 1) * 128], t_id[:]), inc=True)
                        v_tp = rec.val("ct")
                        rec.wait("s", "ct", v_tp)
                        sop(lambda e, g=g, sl=sl, col=i4 * 128:
                            e.copy(out=a_acc[:, g, col:col + 128], in_=p_tp[sl]), inc=True)
                ev_tp[ib] = rec.val("ct")
                v_acc = rec.val("cs")
                rec.wait("g", "cs", v_acc)
                dma("g", "gacc", s_xn[:, :, ib * 512:(ib + 1) * 512].rearrange("g p t -> p g t"), a_acc[:])
                ev_acc_dma[ib] = rec.val("gacc")
                yield

              fence("v")
              vop(lambda e: e.tensor_copy(t_tmpc[:], t_absx[:]), inc=True)
              rec.wait("t", "cv", rec.val("cv"))
              rec.wait("t", "cs", rec.val("cs"))
              top(lambda e: e.transpose(p_tp[0][0:1, :], t_tmpc[:], t_id[:]), inc=True)
              rec.wait("s", "ct", rec.val("ct"))
              sop(lambda e: e.copy(out=t_rowred[:], in_=p_tp[0][0:1, :]), inc=True)
              rec.wait("v", "cs", rec.val("cs"))
              vop(lambda e: e.tensor_reduce(t_scal[:, 0:1], t_rowred[:], AXX, ALU.max), inc=True)
              rec.wait("g", "cv", rec.val("cv"))
              dma("g", "gcc", cc1_in, t_scal[:, 0:1])
              rec.wait("g", "gcc", rec.val("gcc"))

              def f_cc1(e):
                  return e.collective_compute(
                      "AllReduce", ALU.max,
                      replica_groups=[list(range(NCORES))],
                      ins=[cc1_in.opt()], outs=[cc1_out.opt()])
              rec.op("g", f_cc1, inc="cp")
              v_cc1done[0] = rec.val("cp")

            ev_natv = {}
            ev_stripv = {}
            _gi = [0]
            _hi = [0]
            v_cc1done = [None]

            def qw_scales(nat_ap, n_nat, scol, sinv):
                nb = (n_nat + 1) // 2
                for ib4 in range(nb):
                    i0 = ib4 * 2
                    cnt = min(2, n_nat - i0)
                    gi = _gi[0]; _gi[0] += 1
                    npar = gi % 2
                    nsem = "dq1" if npar == 0 else "dq1b"
                    if gi >= 2:
                        rec.wait("sync", "cv", ev_natv[gi - 2])
                    dma("sync", nsem, q_nat[npar][:, 0:cnt, :],
                        nat_ap[i0:i0 + cnt].rearrange("c p d -> p c d"))
                    rec.wait("v", nsem, rec.val(nsem))
                    for i4 in range(cnt):
                        vop(lambda e, p=npar, i4=i4: e.tensor_reduce(
                            t_tmp4[:, i4:i4 + 1], q_nat[p][:, i4, :], AXX, ALU.max, apply_absolute_value=True),
                            inc=(i4 == cnt - 1))
                    ev_natv[gi] = rec.val("cv")
                    fence("v")
                    vop(lambda e, i0=i0, cnt=cnt, sc=scol: e.tensor_single_scalar(
                        sc[:, i0:i0 + cnt], t_tmp4[:, 0:cnt], INV127, ALU.mult))
                    fence("v")
                    vop(lambda e, i0=i0, cnt=cnt, si=sinv, sc=scol: e.reciprocal(
                        si[:, i0:i0 + cnt], sc[:, i0:i0 + cnt]))
                    yield
                fence_inc("v", "cv")

            def qw_scales_wide(nat_list, scol, sinv):
                for icx in range(len(nat_list)):
                    hi = _hi[0]; _hi[0] += 1
                    spar = hi % 4
                    ssem = ("dq2", "dq2b", "dq2c", "dq2d")[spar]
                    if hi >= 4:
                        rec.wait("sync", "cv", ev_stripv[hi - 4])
                    dma("sync", ssem, q_strip[spar][:, 0:I], nat_list[icx])
                    rec.wait("v", ssem, rec.val(ssem))
                    vop(lambda e, p=spar: e.tensor_reduce(t_tmp4[:, 2:3], q_strip[p][:, 0:I], AXX, ALU.max, apply_absolute_value=True), inc=True)
                    ev_stripv[hi] = rec.val("cv")
                    fence("v")
                    vop(lambda e, icx=icx, sc=scol: e.tensor_single_scalar(sc[:, icx:icx + 1], t_tmp4[:, 2:3], INV127, ALU.mult))
                    fence("v")
                    vop(lambda e, icx=icx, si=sinv, sc=scol: e.reciprocal(si[:, icx:icx + 1], sc[:, icx:icx + 1]))
                    yield
                fence_inc("v", "cv")

            def qw_row(sinv, row_idx, n_nat, free_T):
                rec.wait("t", "cv", rec.val("cv"))
                rec.wait("t", "cs", rec.val("cs"))
                top(lambda e, n=n_nat, si=sinv: e.transpose(p_tp[0][0:n, :], si[:, 0:n], t_id[:]), inc=True)
                rec.wait("s", "ct", rec.val("ct"))
                rec.wait("s", "grow", rec.val("grow"))
                sop(lambda e, n=n_nat: e.copy(out=t_sinvT[0:n, :], in_=p_tp[0][0:n, :]), inc=True)
                rec.wait("g", "cs", rec.val("cs"))
                dma("g", "grow", s_rows[row_idx, 0:free_T].rearrange("(c p) -> c p", p=128), t_sinvT[0:n_nat, :])

            def qw_strips(T_list, row_idx, free_T, dst_scatter, q_repx):
                rec.wait("sync", "grow", rec.val("grow"))
                rec.wait("sync", "cv", rec.val("cv"))
                dma("sync", "dq3", q_repx[:, 0:free_T], s_rows[row_idx:row_idx + 1, 0:free_T].broadcast_to((128, free_T)))
                v_rep = rec.val("dq3")
                for hgx in range(len(T_list)):
                    hi = _hi[0]; _hi[0] += 1
                    spar = hi % 4
                    ssem = ("dq2", "dq2b", "dq2c", "dq2d")[spar]
                    if hi >= 4:
                        rec.wait("sync", "cv", ev_stripv[hi - 4])
                    dma("sync", ssem, q_strip[spar][:, 0:free_T], T_list[hgx])
                    rec.wait("v", ssem, rec.val(ssem))
                    rec.wait("v", "dq3", v_rep)
                    rec.wait("v", "cs", rec.val("cs"))
                    vop(lambda e, p=spar, qr=q_repx, fr=free_T: e.tensor_tensor(q_f[:, 0:fr], q_strip[p][:, 0:fr], qr[:, 0:fr], ALU.mult), inc=True)
                    ev_stripv[hi] = rec.val("cv")
                    fence("v")
                    vop(lambda e, fr=free_T: e.tensor_scalar(q_i8[:, 0:fr], q_f[:, 0:fr], 12582912.0, -12582912.0, ALU.add, ALU.add), inc=True)
                    rec.wait("s", "cv", rec.val("cv"))
                    rec.wait("s", "gwq", rec.val("gwq"))
                    sop(lambda e, fr=free_T: e.copy(out=q_b[:, 0:fr], in_=q_i8[:, 0:fr]), inc=True)
                    rec.wait("g", "cs", rec.val("cs"))
                    dst, srcv = dst_scatter(hgx, q_b)
                    dma("g", "gwq", dst, srcv)
                    yield

            def gen_a2():
                yield from qw_scales(d_wg_nat, ICH, t_sg, t_sinv)
                qw_row(t_sinv, 0, ICH, I)
                yield from qw_scales(d_wu_nat, ICH, t_su, t_sinv2)
                qw_row(t_sinv2, 1, ICH, I)
                yield from qw_scales_wide([d_wd_nat[i] for i in range(DCH)], t_sd, t_sinv3)
                qw_row(t_sinv3, 2, DCH, D)
                yield from qw_strips([d_wgT[i] for i in range(DCH)], 0, I,
                          lambda hgx, qb: (s_wqg[:, hgx, :, :].rearrange("c p e -> p c e"),
                                           qb[:, 0:I].rearrange("p (c e) -> p c e", e=128)), q_rep)
                yield from qw_strips([d_wuT[i] for i in range(DCH)], 1, I,
                          lambda hgx, qb: (s_wqu[:, hgx, :, :].rearrange("c p e -> p c e"),
                                           qb[:, 0:I].rearrange("p (c e) -> p c e", e=128)), q_rep)
                yield from qw_strips([d_wdT[i] for i in range(ICH)], 2, D,
                          lambda icx, qb: (s_wqd[icx], qb[:, 0:D]), q_rep)

                for hgx in range(DCH):
                    rec.wait("g", "cv", rec.val("cv"))
                    rec.wait("g", "cs", rec.val("cs"))
                    dma("g", "grow", s_rows[3, hgx * 128:(hgx + 1) * 128].unsqueeze(1), t_sd[:, hgx:hgx + 1])

            _g1, _g2 = gen_ln(), gen_a2()
            _d1 = _d2 = False
            while not (_d1 and _d2):
                if not _d1:
                    try:
                        next(_g1)
                    except StopIteration:
                        _d1 = True
                for _ in range(8):
                    if not _d2:
                        try:
                            next(_g2)
                        except StopIteration:
                            _d2 = True

            rec.wait("sync", "cp", v_cc1done[0])
            dma("sync", "dsc", t_scal[:, 1:2], cc1_out)
            dma("sync", "dsc", t_scal[:, 2:3], d_clipx)
            dma("sync", "dsc", t_scal[:, 3:4], d_clipd)
            v_cc1in = rec.val("dsc")
            rec.wait("s", "dsc", v_cc1in)
            sop(lambda e: e.activation(out=t_scal[:, 4:5], in_=t_scal[:, 2:3], func=ACT.Sigmoid), inc=True)
            rec.wait("v", "cs", rec.val("cs"))
            vop(lambda e: e.tensor_tensor(t_scal[:, 5:6], t_scal[:, 4:5], t_scal[:, 1:2], ALU.mult))
            fence("v")
            vop(lambda e: e.tensor_single_scalar(t_scal[:, 6:7], t_scal[:, 5:6], INV127, ALU.mult))
            fence("v")
            vop(lambda e: e.reciprocal(t_scal[:, 7:8], t_scal[:, 6:7]), inc=True)
            rec.wait("g", "cv", rec.val("cv"))
            rec.wait("g", "grow", rec.val("grow"))
            dma("g", "grow", s_rows[4, 0:2].unsqueeze(0), t_scal[:, 6:8])
            rec.wait("sync", "grow", rec.val("grow"))
            dma("sync", "dsc", t_sxcol[:], s_rows[4:5, 0:1].broadcast_to((128, 1)))
            dma("sync", "dsc", t_invx[:], s_rows[4:5, 1:2].broadcast_to((128, 1)))
            v_invx = rec.val("dsc")
            rec.wait("v", "dsc", v_invx)
            vop(lambda e: e.tensor_scalar(t_sg2[:], t_sg[:], t_sxcol[:], None, ALU.mult))
            vop(lambda e: e.tensor_scalar(t_su2[:], t_su[:], t_sxcol[:], None, ALU.mult), inc=True)

            barrier()

        with contextlib.ExitStack() as esB:
            b_xnb = [mksb(esB, f"b_xnb{i}", [128, DCH, TC + 1]) for i in range(2)]
            b_xm = [mksb(esB, f"b_xm{i}", [128, DCH, TC]) for i in range(2)]
            b_w = [mksb(esB, f"b_w{i}", [128, DCH, 128]) for i in range(2)]
            b_r = [mksb(esB, f"b_r{i}", [128, DCH, TC]) for i in range(2)]
            b_ek = [mksb(esB, f"b_ek{i}", [128, DCH, TC]) for i in range(2)]
            b_v = [mksb(esB, f"b_v{i}", [128, DCH, TC]) for i in range(2)]
            b_td = [mksb(esB, f"b_td{i}", [128, DCH, TC]) for i in range(2)]
            b_u = mksb(esB, "b_u", [128, DCH, TC])
            b_a = mksb(esB, "b_a", [128, DCH, TC])
            b_b = mksb(esB, "b_b", [128, DCH, TC])
            b_acar = mksb(esB, "b_acar", [128, DCH])
            b_bcar = mksb(esB, "b_bcar", [128, DCH])
            b_num = mksb(esB, "b_num", [128, DCH, TC])
            b_den = mksb(esB, "b_den", [128, DCH, TC])
            b_rec = mksb(esB, "b_rec", [128, DCH, TC])
            b_y = [mksb(esB, f"b_y{i}", [128, TC, DCH]) for i in range(2)]
            b_nh = mksb(esB, "b_nh", [128, TC, DCH])
            b_w2 = mksb(esB, "b_w2", [128, DCH])
            b_spk = [mksb(esB, f"b_spk{i}", [128, DCH, TC], BF16) for i in range(2)]

            ev_xm = {}
            ev_group_mm = {}
            ev_evac = {}
            ev_spkdma = {}
            ev_ydone = {}

            def emit_xm(c):
                par = c % 2
                xnsem = "dxn0" if par == 0 else "dxn1"
                if c >= 2:
                    rec.wait("sync", "cv", ev_xm[c - 2])
                if c > 0:
                    dma("sync", xnsem, b_xnb[par][:],
                        s_xn[:, :, c * TC - 1:(c + 1) * TC].rearrange("g p t -> p g t"))
                else:
                    dma("sync", xnsem, b_xnb[0][:, :, 1:TC + 1],
                        s_xn[:, :, 0:TC].rearrange("g p t -> p g t"))
                    vop(lambda e: e.memset(b_xnb[0][:, :, 0:1], 0.0))
                    fence("v")
                v_xnb = rec.val(xnsem)
                rec.wait("v", xnsem, v_xnb)
                if c >= 2:
                    rec.wait("v", "ct", ev_group_mm[(c - 2, 3, DCH - 1)])
                for g in range(DCH):
                    vop(lambda e, g=g, p=par: e.tensor_scalar(
                        b_xm[p][:, g, :], b_xnb[p][:, g, 1:TC + 1], t_mix[:, g:g + 1], None, ALU.mult))
                fence("v")
                for g in range(DCH):
                    vop(lambda e, g=g, p=par: e.scalar_tensor_tensor(
                        b_xm[p][:, g, :], b_xnb[p][:, g, 0:TC], t_imix[:, g:g + 1],
                        b_xm[p][:, g, :], ALU.mult, ALU.add),
                        inc=(g == DCH - 1))
                ev_xm[c] = rec.val("cv")

            def emit_memscan(c):
                par = c % 2
                spsem = "gsp0" if par == 0 else "gsp1"
                if c >= 2:
                    rec.wait("v", spsem, ev_spkdma[c - 2])
                for t in range(TC):
                    if c == 0 and t == 0:
                        src = t_zero8[:]
                    else:
                        src = b_nh[:, t - 1, :] if t > 0 else b_nh[:, TC - 1, :]

                    def f1(e, t=t, s=src, p=par):
                        return e.scalar_tensor_tensor(
                            b_w2[:], s, BETA_, b_y[p][:, t, :], ALU.mult, ALU.add)

                    def f2(e, t=t, s=src):
                        return e.scalar_tensor_tensor(
                            b_nh[:, t, :], s, -1.0, b_w2[:], ALU.is_lt, ALU.add)
                    vop(f1)
                    nfence("v")
                    vop(f2)
                    nfence("v")
                vop(lambda e, p=par: e.tensor_single_scalar(
                    b_spk[p][:], b_nh[:].rearrange("p t g -> p g t"), -1.0, ALU.is_lt), inc=True)
                v_spk = rec.val("cv")
                ev_ydone[c] = v_spk
                rec.wait("g", "cv", v_spk)
                dma("g", spsem, s_spk[:, :, c * TC:(c + 1) * TC].rearrange("g p t -> p g t"), b_spk[par][:])
                ev_spkdma[c] = rec.val(spsem)

            emit_xm(0)
            for c in range(NCH):
                par = c % 2
                if c + 1 < NCH:
                    emit_xm(c + 1)

                for eg in range(DCH):
                    for pj in range(4):
                        k = eg * 4 + pj
                        wpar = k % 2
                        wsem = "dw0" if wpar == 0 else "dw1"
                        gk = c * 32 + k
                        if gk >= 2:
                            pg = gk - 2
                            pc_, pk_ = pg // 32, pg % 32
                            rec.wait("sync", "ct", ev_group_mm[(pc_, pk_ % 4, pk_ // 4)])
                        dma("sync", wsem, b_w[wpar][:], d_wstack[pj, eg])
                        v_w = rec.val(wsem)
                        pbs = pp[pj + 4 * (eg % 2)][:, 0:TC]
                        rec.wait("t", wsem, v_w)
                        rec.wait("t", "cv", ev_xm[c])
                        pkey = (c, pj, eg - 2) if eg > 1 else ((c - 1, pj, eg + DCH - 2) if c > 0 else None)
                        if pkey is not None:
                            eng_s, vv = ev_evac[pkey]
                            rec.wait("t", eng_s, vv)
                        for dg in range(DCH):
                            top(lambda e, pbs=pbs, dg=dg, wp=wpar, xp=par, st=(dg == 0), sp=(dg == DCH - 1):
                                e.matmul(pbs, b_w[wp][:, dg, :], b_xm[xp][:, dg, :],
                                         start=st, stop=sp),
                                inc=(dg == DCH - 1))
                        ev_group_mm[(c, pj, eg)] = rec.val("ct")
                        v_mm = rec.val("ct")
                        v_prevchain = ev_ydone.get(c - 2, 0)
                        if pj == 0:
                            rec.wait("s", "ct", v_mm)
                            rec.wait("s", "cv", v_prevchain)
                            sop(lambda e, eg=eg, p=par, pbs=pbs: e.activation(out=b_r[p][:, eg, :], in_=pbs, func=ACT.Sigmoid), inc=True)
                            ev_evac[(c, pj, eg)] = ("cs", rec.val("cs"))
                        elif pj == 1:
                            rec.wait("s", "ct", v_mm)
                            rec.wait("s", "cv", v_prevchain)
                            sop(lambda e, eg=eg, p=par, pbs=pbs: e.activation(out=b_ek[p][:, eg, :], in_=pbs, func=ACT.Exp), inc=True)
                            ev_evac[(c, pj, eg)] = ("cs", rec.val("cs"))
                        elif pj == 2:
                            rec.wait("s", "ct", v_mm)
                            rec.wait("s", "cv", v_prevchain)
                            sop(lambda e, eg=eg, p=par, pbs=pbs: e.copy(out=b_v[p][:, eg, :], in_=pbs), inc=True)
                            ev_evac[(c, pj, eg)] = ("cs", rec.val("cs"))
                        else:
                            rec.wait("s", "ct", v_mm)
                            rec.wait("s", "cv", v_prevchain)
                            sop(lambda e, eg=eg, p=par, pbs=pbs: e.activation(out=b_td[p][:, eg, :], in_=pbs, func=ACT.Exp), inc=True)
                            ev_evac[(c, pj, eg)] = ("cs", rec.val("cs"))
                    g = eg
                    for pjx in range(4):
                        eng_s, vv = ev_evac[(c, pjx, g)]
                        rec.wait("v", eng_s, vv)
                    if c == 0 and g == 0:
                        rec.wait("v", "cs", ev_gd)
                    ia = t_zero8[:, g:g + 1] if c == 0 else b_acar[:, g:g + 1]
                    ib = t_zero8[:, g:g + 1] if c == 0 else b_bcar[:, g:g + 1]
                    vop(lambda e, g=g, p=par: e.tensor_single_scalar(
                        b_ek[p][:, g, :], b_ek[p][:, g, :], EXP10, ALU.min))
                    fence("v")
                    vop(lambda e, g=g, p=par: e.tensor_tensor(
                        b_u[:, g, :], b_ek[p][:, g, :], b_v[p][:, g, :], ALU.mult))
                    fence("v")
                    vop(lambda e, g=g, p=par, ia=ia: e.tensor_tensor_scan(
                        b_a[:, g, :], b_td[p][:, g, :], b_u[:, g, :], ia, ALU.mult, ALU.add))
                    vop(lambda e, g=g, p=par, ib=ib: e.tensor_tensor_scan(
                        b_b[:, g, :], b_td[p][:, g, :], b_ek[p][:, g, :], ib, ALU.mult, ALU.add))
                    fence("v")
                    vop(lambda e, g=g: e.tensor_copy(b_acar[:, g:g + 1], b_a[:, g, TC - 1:TC]))
                    vop(lambda e, g=g: e.tensor_copy(b_bcar[:, g:g + 1], b_b[:, g, TC - 1:TC]))
                    vop(lambda e, g=g: e.scalar_tensor_tensor(
                        b_num[:, g, :], b_u[:, g, :], t_gd[:, g:g + 1], b_a[:, g, :], ALU.mult, ALU.add))
                    vop(lambda e, g=g, p=par: e.scalar_tensor_tensor(
                        b_den[:, g, :], b_ek[p][:, g, :], t_gd[:, g:g + 1], b_b[:, g, :], ALU.mult, ALU.add))
                    fence("v")
                    vop(lambda e, g=g: e.tensor_single_scalar(b_rec[:, g, :], b_den[:, g, :], EPS, ALU.add))
                    fence("v")
                    vop(lambda e, g=g: e.reciprocal(b_den[:, g, :], b_rec[:, g, :]))
                    fence("v")
                    vop(lambda e, g=g: e.scalar_tensor_tensor(
                        b_rec[:, g, :], b_num[:, g, :], -5.0, b_den[:, g, :], ALU.mult, ALU.mult))
                    fence("v")
                    vop(lambda e, g=g, p=par: e.tensor_tensor(
                        b_y[p][:, :, g], b_rec[:, g, :], b_r[p][:, g, :], ALU.mult),
                        inc=(g == DCH - 1))
                    fence("v")

                emit_memscan(c)

            barrier()

        with contextlib.ExitStack() as esC:
            esC1 = contextlib.ExitStack()
            c_xn = [mksb(esC1, f"c_xn{i}", [128, DCH, FTC]) for i in range(2)]
            c_xq8 = mksb(esC1, "c_xq8", [128, DCH, FTC], I8)
            c_xf = mksb(esC1, "c_xf", [128, DCH, FTC])
            c_xqb = [mksb(esC1, f"c_xqb{i}", [128, DCH, FTC], BF16) for i in range(2)]
            c_wg = [mksb(esC1, f"c_wg{i}", [128, DCH, 128], BF16) for i in range(2)]
            c_wu = [mksb(esC1, f"c_wu{i}", [128, DCH, 128], BF16) for i in range(2)]
            c_sg = mksb(esC1, "c_sg", [128, FTC])
            c_su = mksb(esC1, "c_su", [128, FTC])
            c_h = [mksb(esC1, f"c_h{i}", [128, FTC]) for i in range(2)]

            ev_xq = {}
            ev_mm1 = {}
            ev_evac1 = {}
            ev_h = {}
            ev_hdma = {}
            def emit_xq(fc):
                par = fc % 2
                cxsem = "dcx0" if par == 0 else "dcx1"
                if fc >= 2:
                    rec.wait("sync", "cs", ev_xq[fc - 2])
                dma("sync", cxsem, c_xn[par][:], s_xn[:, :, fc * FTC:(fc + 1) * FTC].rearrange("g p t -> p g t"))
                v_cxn = rec.val(cxsem)
                if fc >= 2:
                    rec.wait("s", "ct", ev_mm1[(fc - 2, ICH - 1)])
                rec.wait("v", cxsem, v_cxn)
                rec.wait("v", "dsc", v_invx)
                if fc >= 2:
                    rec.wait("v", "ct", ev_mm1[(fc - 2, ICH - 1)])
                if fc >= 1:
                    rec.wait("v", "cs", ev_xq[fc - 1])
                vop(lambda e, p=par: e.tensor_scalar(c_xf[:], c_xn[p][:], t_invx[:], 12582912.0, ALU.mult, ALU.add))
                fence("v")
                vop(lambda e: e.tensor_scalar(c_xf[:], c_xf[:], -12582912.0, -128.0, ALU.add, ALU.max))
                fence("v")
                vop(lambda e: e.tensor_single_scalar(c_xq8[:], c_xf[:], 127.0, ALU.min), inc=True)
                rec.wait("s", "cv", rec.val("cv"))
                sop(lambda e, p=par: e.copy(out=c_xqb[p][:], in_=c_xq8[:]), inc=True)
                ev_xq[fc] = rec.val("cs")

            emit_xq(0)
            for fc in range(NFC):
                par = fc % 2
                v_xq = ev_xq[fc]
                if fc + 1 < NFC:
                    emit_xq(fc + 1)

                for ic in range(ICH):
                    kk = fc * ICH + ic
                    wpar = kk % 2
                    wgsem = "dwg0" if wpar == 0 else "dwg1"
                    if kk >= 2:
                        pk = kk - 2
                        rec.wait("sync", "ct", ev_mm1[(pk // ICH, pk % ICH)])
                    dma("sync", wgsem, c_wg[wpar][:], s_wqg[ic].rearrange("d p e -> p d e"))
                    dma("sync", wgsem, c_wu[wpar][:], s_wqu[ic].rearrange("d p e -> p d e"))
                    v_wq = rec.val(wgsem)
                    pslot = kk % 2
                    pgt = p_c[0] if pslot == 0 else p_c[2]
                    put = p_c[1] if pslot == 0 else p_c[3]
                    rec.wait("t", wgsem, v_wq)
                    rec.wait("t", "cs", v_xq)
                    if kk >= 2:
                        pk = kk - 2
                        rec.wait("t", "cs", ev_evac1[(pk // ICH, pk % ICH)])
                    for dg in range(DCH):
                        top(lambda e, pgt=pgt, dg=dg, wp=wpar, xp=par, st=(dg == 0), sp=(dg == DCH - 1):
                            e.matmul(pgt[:], c_wg[wp][:, dg, :], c_xqb[xp][:, dg, :], start=st, stop=sp))
                    for dg in range(DCH):
                        top(lambda e, put=put, dg=dg, wp=wpar, xp=par, st=(dg == 0), sp=(dg == DCH - 1):
                            e.matmul(put[:], c_wu[wp][:, dg, :], c_xqb[xp][:, dg, :], start=st, stop=sp),
                            inc=(dg == DCH - 1))
                    ev_mm1[(fc, ic)] = rec.val("ct")
                    v_mm2 = rec.val("ct")
                    rec.wait("s", "ct", v_mm2)
                    if kk >= 1:
                        pk = kk - 1
                        rec.wait("s", "cv", ev_h[(pk // ICH, pk % ICH)])
                    sop(lambda e, ic=ic, pgt=pgt: e.activation(out=c_sg[:], in_=pgt[:], func=ACT.Silu, scale=t_sg2[:, ic:ic + 1]))
                    sop(lambda e, ic=ic, put=put: e.activation(out=c_su[:], in_=put[:], func=ACT.Copy, scale=t_su2[:, ic:ic + 1]), inc=True)
                    ev_evac1[(fc, ic)] = rec.val("cs")
                    v_ev = rec.val("cs")
                    hpar = kk % 2
                    hsem = "gh0" if hpar == 0 else "gh1"
                    rec.wait("v", "cs", v_ev)
                    if kk >= 2:
                        rec.wait("v", hsem, ev_hdma[kk - 2])
                    vop(lambda e, hp=hpar: e.tensor_tensor(c_h[hp][:], c_sg[:], c_su[:], ALU.mult), inc=True)
                    ev_h[(fc, ic)] = rec.val("cv")
                    fence("v")
                    vop(lambda e, hp=hpar: e.tensor_reduce(t_tmpc[:], c_h[hp][:], AXX, ALU.max, apply_absolute_value=True))
                    fence("v")
                    vop(lambda e: e.tensor_tensor(t_absh[:], t_absh[:], t_tmpc[:], ALU.max), inc=True)
                    rec.wait("g", "cv", rec.val("cv"))
                    j0 = fc * (FTC // 128)
                    dma("g", hsem, s_h[j0:j0 + 4, :, ic, :].rearrange("j p t -> p j t"),
                        c_h[hpar][:].rearrange("p (j t) -> p j t", t=128))
                    ev_hdma[kk] = rec.val(hsem)

            barrier()
            esC1.close()
            c_hre = [mksb(esC, f"c_hre{i}", [128, ICH, 128]) for i in range(2)]
            c_hq8 = mksb(esC, "c_hq8", [128, ICH, 128], I8)
            c_hf = mksb(esC, "c_hf", [128, ICH, 128])
            c_hqb = [mksb(esC, f"c_hqb{i}", [128, ICH, 128], BF16) for i in range(2)]
            c_wdr = mksb(esC, "c_wdr", [128, ICH, D], BF16)
            c_sdrep = mksb(esC, "c_sdrep", [128, D])
            c_spkT = [mksb(esC, f"c_spkT{i}", [128, D], BF16) for i in range(2)]
            c_spkF = [mksb(esC, f"c_spkF{i}", [128, D]) for i in range(2)]
            c_xres = [mksb(esC, f"c_xres{i}", [128, D]) for i in range(2)]
            c_xs = [mksb(esC, f"c_xs{i}", [128, D]) for i in range(2)]
            c_t1 = mksb(esC, "c_t1", [128, D])
            c_out = [mksb(esC, f"c_out{i}", [128, D]) for i in range(2)]

            for ic in range(ICH):
                dma("sync", "dwd0", c_wdr[:, ic, :], s_wqd[ic])
            v_wdres = rec.val("dwd0")

            fence("v")
            vop(lambda e: e.tensor_copy(t_tmpc[:], t_absh[:]), inc=True)
            rec.wait("t", "cv", rec.val("cv"))
            rec.wait("t", "cs", rec.val("cs"))
            top(lambda e: e.transpose(p_tp[0][0:1, :], t_tmpc[:], t_id[:]), inc=True)
            rec.wait("s", "ct", rec.val("ct"))
            sop(lambda e: e.copy(out=t_rowred[:], in_=p_tp[0][0:1, :]), inc=True)
            rec.wait("v", "cs", rec.val("cs"))
            vop(lambda e: e.tensor_reduce(t_scal[:, 8:9], t_rowred[:], AXX, ALU.max), inc=True)
            rec.wait("g", "cv", rec.val("cv"))
            dma("g", "gcc", cc2_in, t_scal[:, 8:9])
            rec.wait("g", "gcc", rec.val("gcc"))

            def f_cc2(e):
                return e.collective_compute(
                    "AllReduce", ALU.max,
                    replica_groups=[list(range(NCORES))],
                    ins=[cc2_in.opt()], outs=[cc2_out.opt()])
            rec.op("g", f_cc2, inc="cp")
            rec.wait("sync", "cp", rec.val("cp"))
            dma("sync", "dsc", t_scal[:, 9:10], cc2_out)
            v_cc2 = rec.val("dsc")
            rec.wait("s", "dsc", v_cc2)
            sop(lambda e: e.activation(out=t_scal[:, 10:11], in_=t_scal[:, 3:4], func=ACT.Sigmoid), inc=True)
            rec.wait("v", "cs", rec.val("cs"))
            vop(lambda e: e.tensor_tensor(t_scal[:, 11:12], t_scal[:, 10:11], t_scal[:, 9:10], ALU.mult))
            fence("v")
            vop(lambda e: e.tensor_single_scalar(t_scal[:, 12:13], t_scal[:, 11:12], INV127, ALU.mult))
            fence("v")
            vop(lambda e: e.reciprocal(t_scal[:, 13:14], t_scal[:, 12:13]), inc=True)
            rec.wait("g", "cv", rec.val("cv"))
            rec.wait("g", "gcc", rec.val("gcc"))
            dma("g", "gcc", s_rows[4, 2:4].unsqueeze(0), t_scal[:, 12:14])
            rec.wait("sync", "gcc", rec.val("gcc"))
            dma("sync", "dsc", t_shcol[:], s_rows[4:5, 2:3].broadcast_to((128, 1)))
            dma("sync", "dsc", t_invh[:], s_rows[4:5, 3:4].broadcast_to((128, 1)))
            dma("sync", "dsc", c_sdrep[:], s_rows[3:4, 0:D].broadcast_to((128, D)))
            v_sdr = rec.val("dsc")
            v_invh = rec.val("dsc")
            rec.wait("v", "dsc", v_sdr)
            vop(lambda e: e.tensor_scalar(c_sdrep[:], c_sdrep[:], t_shcol[:], None, ALU.mult), inc=True)
            v_sdrep = rec.val("cv")

            rec.wait("sync", "gh0", rec.val("gh0"))
            rec.wait("sync", "gh1", rec.val("gh1"))
            rec.wait("sync", "gsp0", rec.val("gsp0"))
            rec.wait("sync", "gsp1", rec.val("gsp1"))
            ev_hq = {}
            ev_spkf = {}
            ev_out = {}
            ev_mm2w = {}
            ev_devac = {}
            ev_odma = {}
            ev_xs = {}

            def emit_prep(j):
                par = j % 2
                hrsem = "dhr0" if par == 0 else "dhr1"
                spsem2 = "dsp0" if par == 0 else "dsp1"
                if j >= 2:
                    rec.wait("sync", "cs", ev_hq[j - 2])
                dma("sync", hrsem, c_hre[par][:], s_h[j])
                v_hre = rec.val(hrsem)
                if j >= 2:
                    rec.wait("sync", "cs", ev_spkf[j - 2])
                    rec.wait("sync", "cv", ev_out[j - 2])
                for g in range(DCH):
                    dma("sync", spsem2,
                        c_spkT[par][:, g * 128:(g + 1) * 128],
                        s_spk[g, :, j * 128:(j + 1) * 128],
                        transpose=True)
                dma("sync", spsem2, c_xres[par][:], d_x[j * 128:(j + 1) * 128, :])
                v_in2 = rec.val(spsem2)
                if j >= 2:
                    rec.wait("s", "ct", ev_mm2w[(j - 2, ICH - 1)])
                rec.wait("v", hrsem, v_hre)
                rec.wait("v", "dsc", v_invh)
                if j >= 2:
                    rec.wait("v", "ct", ev_mm2w[(j - 2, ICH - 1)])
                if j >= 1:
                    rec.wait("v", "cs", ev_hq[j - 1])
                vop(lambda e, p=par: e.tensor_scalar(c_hf[:], c_hre[p][:], t_invh[:], 12582912.0, ALU.mult, ALU.add))
                fence("v")
                vop(lambda e: e.tensor_scalar(c_hf[:], c_hf[:], -12582912.0, -128.0, ALU.add, ALU.max))
                fence("v")
                vop(lambda e: e.tensor_single_scalar(c_hq8[:], c_hf[:], 127.0, ALU.min), inc=True)
                rec.wait("s", "cv", rec.val("cv"))
                sop(lambda e, p=par: e.copy(out=c_hqb[p][:], in_=c_hq8[:]), inc=True)
                ev_hq[j] = rec.val("cs")
                rec.wait("s", spsem2, v_in2)
                sop(lambda e, p=par: e.copy(out=c_spkF[p][:], in_=c_spkT[p][:]), inc=True)
                ev_spkf[j] = rec.val("cs")
                v_spkf = rec.val("cs")
                rec.wait("v", "cs", v_spkf)
                rec.wait("v", spsem2, v_in2)
                if j >= 2:
                    rec.wait("v", "cv", ev_out[j - 2])
                vop(lambda e, p=par: e.tensor_tensor(c_xs[p][:], c_xres[p][:], c_spkF[p][:], ALU.add), inc=True)
                ev_xs[j] = rec.val("cv")

            emit_prep(0)
            for j in range(TTILES):
                par = j % 2
                osem = "go0" if par == 0 else "go1"
                v_hq = ev_hq[j]
                v_xs = ev_xs[j]
                if j + 1 < TTILES:
                    emit_prep(j + 1)

                pdA = p_c[2] if par == 0 else pp[4]
                pdB = p_c[3] if par == 0 else pp[5]
                rec.wait("t", "dwd0", v_wdres)
                rec.wait("t", "cs", v_hq)
                if j >= 2:
                    rec.wait("t", "cv", ev_devac[j - 2])
                for ic in range(ICH):
                    top(lambda e, pdA=pdA, ic=ic, hp=par, st=(ic == 0), sp=(ic == ICH - 1):
                        e.matmul(pdA[:], c_hqb[hp][:, ic, :], c_wdr[:, ic, 0:512],
                                 start=st, stop=sp, skip_group_check=True))
                    top(lambda e, pdB=pdB, ic=ic, hp=par, st=(ic == 0), sp=(ic == ICH - 1):
                        e.matmul(pdB[:], c_hqb[hp][:, ic, :], c_wdr[:, ic, 512:D],
                                 start=st, stop=sp, skip_group_check=True),
                        inc=True)
                    ev_mm2w[(j, ic)] = rec.val("ct")
                v_dmm = rec.val("ct")
                rec.wait("v", "ct", v_dmm)
                rec.wait("v", "cv", max(v_sdrep, v_xs))
                fence("v")
                vop(lambda e, pdA=pdA: e.tensor_tensor(c_t1[:, 0:512], pdA[:], c_sdrep[:, 0:512], ALU.mult))
                vop(lambda e, pdB=pdB: e.tensor_tensor(c_t1[:, 512:D], pdB[:], c_sdrep[:, 512:D], ALU.mult), inc=True)
                ev_devac[j] = rec.val("cv")
                fence("v")
                if j >= 2:
                    rec.wait("v", osem, ev_odma[j - 2])
                vop(lambda e, p=par: e.tensor_tensor(c_out[p][:], c_t1[:], c_xs[p][:], ALU.add), inc=True)
                ev_out[j] = rec.val("cv")
                rec.wait("g", "cv", ev_out[j])
                dma("g", osem, d_out[j * 128:(j + 1) * 128, :], c_out[par][:])
                ev_odma[j] = rec.val(osem)

            rec.wait("g", "go0", rec.val("go0"))
            rec.wait("g", "go1", rec.val("go1"))

        @block.sync
        def _(sync):
            rec.replay("sync", sync, handles)

        @block.vector
        def _(vector):
            rec.replay("v", vector, handles)

        @block.scalar
        def _(scalar):
            rec.replay("s", scalar, handles)

        @block.tensor
        def _(tensor):
            rec.replay("t", tensor, handles)

        @block.gpsimd
        def _(gpsimd):
            rec.replay("g", gpsimd, handles)

    return nc


def kernel(L0_attn_embeddings, L0_attn_W_R, L0_attn_W_K, L0_attn_W_V, L0_attn_W_td,
           L0_attn_global_decay, L0_attn_mix, L0_ffn_clip_x, L0_ffn_clip_down,
           L0_ffn_w_gate, L0_ffn_w_up, L0_ffn_w_down):
    x = np.ascontiguousarray(np.asarray(L0_attn_embeddings, dtype=np.float32))

    def wtiles(w):
        w = np.asarray(w, dtype=np.float32)
        return np.ascontiguousarray(w.reshape(DCH, 128, DCH, 128).transpose(0, 3, 2, 1))

    wstack = np.ascontiguousarray(np.stack([
        wtiles(L0_attn_W_R), wtiles(L0_attn_W_K),
        wtiles(L0_attn_W_V), wtiles(L0_attn_W_td)], axis=0))
    gd_col = np.ascontiguousarray(np.asarray(L0_attn_global_decay, np.float32).reshape(DCH, 128).T)
    mix_col = np.ascontiguousarray(np.asarray(L0_attn_mix, np.float32).reshape(DCH, 128).T)
    wg = np.asarray(L0_ffn_w_gate, np.float32)
    wu = np.asarray(L0_ffn_w_up, np.float32)
    wd = np.asarray(L0_ffn_w_down, np.float32)
    wg_nat = np.ascontiguousarray(wg.reshape(ICH, 128, D))
    wu_nat = np.ascontiguousarray(wu.reshape(ICH, 128, D))
    wd_nat = np.ascontiguousarray(wd.reshape(DCH, 128, I))
    wgT = np.ascontiguousarray(wg.T).reshape(DCH, 128, I)
    wuT = np.ascontiguousarray(wu.T).reshape(DCH, 128, I)
    wdT = np.ascontiguousarray(wd.T).reshape(ICH, 128, D)
    clipx = np.asarray(L0_ffn_clip_x, np.float32).reshape(1, 1)
    clipd = np.asarray(L0_ffn_clip_down, np.float32).reshape(1, 1)
    idm = np.eye(128, dtype=np.float32)

    nc = build_program()
    in_maps = []
    for b in range(NCORES):
        in_maps.append({
            "x": np.ascontiguousarray(x[b]),
            "wstack": wstack, "gdraw": gd_col, "mixv": mix_col,
            "clipx": clipx, "clipd": clipd,
            "wg_nat": wg_nat, "wu_nat": wu_nat, "wd_nat": wd_nat,
            "wgT": wgT, "wuT": wuT, "wdT": wdT, "idm": idm,
        })
    res = run_bass_kernel_spmd(nc, in_maps, core_ids=list(range(NCORES)))
    out = np.stack([np.asarray(res.results[b]["out"]) for b in range(NCORES)], axis=0)
    return out
